# version 18
# baseline (speedup 1.0000x reference)
import numpy as np

import concourse.bass as bass
import concourse.mybir as mybir
from concourse.bass_utils import run_bass_kernel_spmd

F32 = mybir.dt.float32
BF16 = mybir.dt.bfloat16
I32 = mybir.dt.int32
AF = mybir.ActivationFunctionType
ALU = mybir.AluOpType

D = 4096
DC = 32
FK = 1024
FKH = 512
HKV = 8
FF = 11008
FFC = 86
L = 4096
OWN = 2048
NBLK = 32
NOWNB = 16
P2H = 1024
F2T = 512
NT2 = 4
EPS_LN = 1e-5
EPS_ATT = 1e-10
B = 4
NCORES = 8
ARENA_N = 96256


def prep_x_s1(xb, half):
    v = xb.reshape(NBLK, 128, DC, 128).transpose(0, 3, 2, 1)
    v = np.ascontiguousarray(v).reshape(NBLK, 128, DC * 128)
    if half == 1:
        v = np.ascontiguousarray(np.concatenate([v[NOWNB:], v[:NOWNB]], axis=0))
    return v


def prep_x_p2(xb, half):
    off = half * OWN
    v = xb[off:off + OWN].reshape(2, P2H, DC, 128).transpose(0, 3, 2, 1)
    return np.ascontiguousarray(v).reshape(2, 128, DC * P2H)


def prep_w_fc(w):
    Fo, Din = w.shape
    nfc, ndc = Fo // 128, Din // 128
    v = w.reshape(nfc, 128, ndc, 128).transpose(0, 3, 2, 1)
    return np.ascontiguousarray(v).reshape(nfc, 128, ndc * 128)


def prep_wkv(w):
    v = w.reshape(FK, DC, 128).transpose(2, 1, 0)
    return np.ascontiguousarray(v).reshape(128, DC * FK)


def prep_wd(w):
    v = w.reshape(DC, 128, FFC, 128).transpose(0, 3, 2, 1)
    return np.ascontiguousarray(v).reshape(DC, 128, FFC * 128)


def prep_col(vec):
    n = vec.shape[0] // 128
    return np.ascontiguousarray(vec.reshape(n, 128).T)


def gather_out(parts):
    out = np.empty((B, L, D), np.float32)
    for core in range(NCORES):
        b, half = core // 2, core % 2
        a = parts[core].reshape(NT2, 128, DC, F2T).transpose(0, 3, 2, 1)
        out[b, half * OWN:(half + 1) * OWN, :] = a.reshape(OWN, D)
    return out


CPU_ENG = ("pe", "act", "dve")
DMA_ENG = ("sp", "gp")
ALL_ENG = CPU_ENG + DMA_ENG


class TG:
    def __init__(self, sems):
        self.sems = sems
        self.count = {k: 0 for k in ALL_ENG}
        self.writer = {}
        self.readers = {}
        self.tasks = []
        self.floor = {k: 0 for k in ALL_ENG}
        self.seen = {k: {} for k in ALL_ENG}

    def _aw(self, waits, src, key):
        skey, sval = src
        if skey == key and skey in CPU_ENG:
            return
        waits[skey] = max(waits.get(skey, 0), sval)

    def barrier(self):
        self.floor = {k: self.count[k] for k in ALL_ENG}

    def task(self, key, emit, reads=(), writes=(), tag=""):
        waits = {}
        for r in reads:
            w = self.writer.get(r)
            if w:
                self._aw(waits, w, key)
        for r in writes:
            w = self.writer.get(r)
            if w:
                self._aw(waits, w, key)
            for rd in self.readers.get(r, ()):
                self._aw(waits, rd, key)
        for fk, fv in self.floor.items():
            if fv > 0 and not (fk == key and fk in CPU_ENG):
                waits[fk] = max(waits.get(fk, 0), fv)
        seen = self.seen[key]
        waits = {k: v for k, v in waits.items() if v > seen.get(k, 0)}
        for k, v in waits.items():
            seen[k] = v
        inc = 16 if key in DMA_ENG else 1
        self.count[key] += inc
        me = (key, self.count[key])
        for r in reads:
            self.readers.setdefault(r, []).append(me)
        for r in writes:
            self.writer[r] = me
            self.readers[r] = []
        self.tasks.append((key, waits, emit, tag))

    def seed(self, new_res, old_res):
        sds = self.readers.setdefault(new_res, [])
        for o in old_res:
            w = self.writer.get(o)
            if w:
                sds.append(w)
            sds.extend(self.readers.get(o, ()))

    def emit_into(self, block):
        sems, tasks, count = self.sems, self.tasks, dict(self.count)

        def runner(key):
            def run(eng):
                for (k, waits, emit, tag) in tasks:
                    if k != key:
                        continue
                    for skey in sorted(waits):
                        eng.wait_ge(sems[skey], waits[skey])
                    last = emit(eng)
                    assert last is not None, f"task {tag} emitted nothing"
                    last.then_inc(sems[key], 16 if key in DMA_ENG else 1)
                if key == "sp":
                    eng.wait_ge(sems["sp"], count["sp"])
                    eng.wait_ge(sems["gp"], count["gp"])
            return run

        block.tensor(runner("pe"))
        block.scalar(runner("act"))
        block.vector(runner("dve"))
        block.sync(runner("sp"))
        block.gpsimd(runner("gp"))


class Arena:
    def __init__(self, tg, tensor):
        self.tg = tg
        self.t = tensor
        self.top = 0
        self.hist = []
        self.gen = 0

    def reset(self):
        self.gen += 1
        self.top = 0

    def mark(self):
        return (self.top + 15) // 16 * 16

    def seek(self, off):
        self.top = off

    def take(self, name, nels, dtype=BF16, parts=128):
        off = (self.top + 15) // 16 * 16
        end = off + nels
        assert end <= ARENA_N, f"arena overflow at {name}: {end}"
        self.top = end
        olds = [h[0] for h in self.hist
                if h[3] < self.gen and not (h[2] <= off or h[1] >= end)]
        if olds:
            self.tg.seed(name, olds)
        self.hist.append((name, off, end, self.gen))
        ap = self.t[0:parts, off:off + nels]
        if dtype == F32:
            assert nels % 2 == 0 and off % 2 == 0
            ap = ap.bitcast(F32)
        return ap


def build_program(debug=False):
    nc = bass.Bass("TRN2", target_bir_lowering=False)

    xs1 = nc.dram_tensor("xs1", [NBLK, 128, DC * 128], F32, kind="ExternalInput").ap()
    xp2 = nc.dram_tensor("xp2", [2, 128, DC * P2H], F32, kind="ExternalInput").ap()
    wq_t = nc.dram_tensor("wq_t", [DC, 128, D], F32, kind="ExternalInput").ap()
    wo_t = nc.dram_tensor("wo_t", [DC, 128, D], F32, kind="ExternalInput").ap()
    wk_t = nc.dram_tensor("wk_t", [128, DC * FK], F32, kind="ExternalInput").ap()
    wv_t = nc.dram_tensor("wv_t", [128, DC * FK], F32, kind="ExternalInput").ap()
    gw_t = nc.dram_tensor("gw_t", [FFC, 128, D], I32, kind="ExternalInput").ap()
    uw_t = nc.dram_tensor("uw_t", [FFC, 128, D], I32, kind="ExternalInput").ap()
    dw_t = nc.dram_tensor("dw_t", [DC, 128, FF], I32, kind="ExternalInput").ap()
    bq_c = nc.dram_tensor("bq_c", [128, DC], F32, kind="ExternalInput").ap()
    bo_c = nc.dram_tensor("bo_c", [128, DC], F32, kind="ExternalInput").ap()
    bk_r = nc.dram_tensor("bk_r", [1, FK], F32, kind="ExternalInput").ap()
    bv_r = nc.dram_tensor("bv_r", [1, FK], F32, kind="ExternalInput").ap()
    gs_c = nc.dram_tensor("gs_c", [128, FFC], F32, kind="ExternalInput").ap()
    us_c = nc.dram_tensor("us_c", [128, FFC], F32, kind="ExternalInput").ap()
    ds_c = nc.dram_tensor("ds_c", [128, DC], F32, kind="ExternalInput").ap()

    sk = "ExternalOutput" if debug else "Internal"
    xcc = nc.dram_tensor("xcc", [2, 128, DC * P2H], BF16, kind="Internal").ap()
    x2c = nc.dram_tensor("x2c", [DC, 128, OWN], F32, kind=sk).ap()
    xh2c = nc.dram_tensor("xh2c", [DC, 128, OWN], BF16, kind=sk).ap()
    hc = nc.dram_tensor("hc", [FFC, 128, OWN], BF16, kind=sk).ap()
    if debug:
        dbg_kvt = nc.dram_tensor("dbg_kvt", [128, FK], BF16, kind="ExternalOutput").ap()
        dbg_rows = nc.dram_tensor("dbg_rows", [1, 2 * OWN], BF16, kind="ExternalOutput").ap()
        dbg_recip = nc.dram_tensor("dbg_recip", [1, FK], BF16, kind="ExternalOutput").ap()
        dbg_attn = nc.dram_tensor("dbg_attn", [128, DC * P2H], BF16, kind="ExternalOutput").ap()
        dbg_oh = nc.dram_tensor("dbg_oh", [128, DC * P2H], BF16, kind="ExternalOutput").ap()
        dbg_qh = nc.dram_tensor("dbg_qh", [128, P2H], BF16, kind="ExternalOutput").ap()
        dbg_gw = nc.dram_tensor("dbg_gw", [128, 2 * D], BF16, kind="ExternalOutput").ap()
    dwc = nc.dram_tensor("dwc", [DC, 128, FF], BF16, kind="Internal").ap()
    outp = nc.dram_tensor("outp", [NT2, 128, DC * F2T], F32,
                          kind="ExternalOutput").ap()

    from contextlib import ExitStack
    st = ExitStack()
    T = lambda *a: st.enter_context(nc.sbuf_tensor(*a))

    arena_t = T("arena", [128, ARENA_N], BF16)
    own_invs = T("own_invs", [1, OWN], BF16)
    trow = T("trow", [1, 128], BF16)
    trow2p = [T("trow2a", [1, 128], BF16), T("trow2b", [1, 128], BF16)]
    trow3 = T("trow3", [1, 128], BF16)
    fr1 = T("fr1", [1, 128], F32)
    fr2 = T("fr2", [1, 128], F32)
    fr3 = T("fr3", [1, 128], F32)
    fr4 = T("fr4", [1, 128], F32)
    recipc = T("recipc", [128, HKV], F32)
    kvt = T("kvt", [128, FK], BF16)
    bqc = T("bqc", [128, DC], F32)
    bqnc = T("bqnc", [128, DC], F32)
    boc = T("boc", [128, DC], F32)
    gsc = T("gsc", [128, FFC], F32)
    usc = T("usc", [128, FFC], F32)
    dsc = T("dsc", [128, DC], F32)
    onesc = T("onesc", [128, 1], BF16)
    negmB = T("negmB", [128, 128], BF16)
    onesr = T("onesr", [1, 128], BF16)
    invscol = T("invscol", [128, 1], F32)
    ninvscol = T("ninvscol", [128, 1], F32)
    epsln = T("epsln", [1, 1], F32)
    epsat = T("epsat", [1, 1], F32)

    PB = [st.enter_context(nc.psum_tensor(f"pb{i}", [128, 512], F32))
          for i in range(8)]
    PBF = [p[:, :] for p in PB]
    sems = {k: st.enter_context(nc.semaphore(f"sem_{k}")) for k in ALL_ENG}
    blk = st.enter_context(nc.Block())
    tg = TG(sems)
    al = Arena(tg, arena_t)

    def mm(t, out, lhsT, rhs, start, stop):
        return t.matmul(out, lhsT, rhs, start=start, stop=stop,
                        skip_group_check=True)

    wk_i = al.take("wk_i", DC * FK)
    wv_i = al.take("wv_i", DC * FK)
    xph = [al.take(f"xph{i}", DC * 128) for i in range(2)]
    sqb = [al.take(f"sqb{i}", 2048) for i in range(2)]
    kvb_k = al.take("kvb_k", FK)
    kvb_v = al.take("kvb_v", FK)
    xcb = [al.take(f"xcb{i}", 4096) for i in range(2)]
    kr = al.take("kr", 1024, F32)
    krn = al.take("krn", 1024, F32)
    ke = al.take("ke", 1024, F32)
    bkr = al.take("bkr", FK, parts=1)
    bvr = al.take("bvr", FK, parts=1)
    ksum = al.take("ksum", 2 * FK, F32, parts=1)
    reciprow = al.take("reciprow", FK, parts=1)

    tg.task("dve", lambda e: e.memset(onesc[:, :], 1.0), writes=["onesc"], tag="oc")
    tg.task("dve", lambda e: e.memset(onesr[:, :], 1.0), writes=["onesr"], tag="or")
    tg.task("dve", lambda e: e.memset(ksum[:, :], 0.0), writes=["ksum"], tag="ks0")
    tg.task("dve", lambda e: e.memset(epsln[:, :], EPS_LN), writes=["epsc"], tag="ep1")
    tg.task("dve", lambda e: e.memset(epsat[:, :], EPS_ATT), writes=["epsc"], tag="ep2")
    tg.task("gp", lambda e: e.dma_start(out=wk_i, in_=wk_t[:, :]), writes=["wk_i"],
            tag="lwk")
    tg.task("gp", lambda e: e.dma_start(out=wv_i, in_=wv_t[:, :]), writes=["wv_i"],
            tag="lwv")
    tg.task("gp", lambda e: e.dma_start(out=bkr, in_=bk_r[:, :]), writes=["bkr"],
            tag="lbk")
    tg.task("gp", lambda e: e.dma_start(out=bvr, in_=bv_r[:, :]), writes=["bvr"],
            tag="lbv")
    tg.task("sp", lambda e: e.dma_start(out=bqc[:, :], in_=bq_c[:, :]),
            writes=["bqc"], tag="lbq")
    tg.task("sp", lambda e: e.dma_start(out=boc[:, :], in_=bo_c[:, :]),
            writes=["boc"], tag="lbo")
    tg.task("sp", lambda e: e.dma_start(out=gsc[:, :], in_=gs_c[:, :]),
            writes=["gsc"], tag="lgs")
    tg.task("sp", lambda e: e.dma_start(out=usc[:, :], in_=us_c[:, :]),
            writes=["usc"], tag="lus")
    tg.task("sp", lambda e: e.dma_start(out=dsc[:, :], in_=ds_c[:, :]),
            writes=["dsc"], tag="lds")
    tg.task("act", lambda e: e.activation(bqnc[:, :], bqc[:, :], AF.Copy,
                                          scale=-1.0),
            reads=["bqc"], writes=["bqnc"], tag="nbq")


    def s1_load(b2):
        return lambda e: e.dma_start(out=xph[b2 % 2], in_=xs1[b2, :, :])

    def s1_sq(b2, hlf):
        par = b2 % 2
        sl = slice(hlf * 2048, (hlf + 1) * 2048)
        return lambda e: e.tensor_tensor(sqb[hlf], xph[par][:, sl], xph[par][:, sl],
                                         ALU.mult)

    def s1_stats(b2, hlf):
        par = b2 % 2
        def f(t):
            last = None
            for j in range(16):
                dc = hlf * 16 + j
                mm(t, PB[0][0:1, 0:128], onesc[:, :],
                   xph[par][:, dc * 128:(dc + 1) * 128], hlf == 0 and j == 0, False)
                last = mm(t, PB[0][0:1, 128:256], onesc[:, :],
                          sqb[hlf][:, j * 128:(j + 1) * 128], False,
                          hlf == 1 and j == 15)
            return last
        return f

    def s1_rows_a(e):
        e.activation(trow[:, :], PB[0][0:1, 0:128], AF.Copy, scale=-1.0 / D)
        e.activation(fr1[:, :], PB[0][0:1, 0:128], AF.Copy, scale=1.0 / D)
        return e.activation(fr2[:, :], PB[0][0:1, 128:256], AF.Copy, scale=1.0 / D)

    def s1_rows_b(e):
        e.tensor_tensor(fr3[:, :], fr1[:, :], fr1[:, :], ALU.mult)
        return e.tensor_tensor(fr4[:, :], fr2[:, :], fr3[:, :], ALU.subtract)

    def s1_rows_c(par):
        def f(e):
            e.activation(fr3[:, :], fr4[:, :], AF.Ln, bias=epsln[:, :])
            e.activation(trow2p[par][:, :], fr3[:, :], AF.Exp, scale=0.5)
            return e.activation(trow3[:, :], fr3[:, :], AF.Exp, scale=-0.5)
        return f

    def s1_rows_own(b2):
        o = b2 * 128
        def f(e):
            return e.activation(own_invs[0:1, o:o + 128], trow3[:, :], AF.Copy)
        return f

    def s1_invcol(t):
        return mm(t, PB[0][:, 0:1], trow3[:, :], onesr[0:1, 0:1], True, True)

    def s1_invev(e):
        e.activation(invscol[:, :], PB[0][:, 0:1], AF.Copy)
        return e.activation(ninvscol[:, :], PB[0][:, 0:1], AF.Copy, scale=-1.0)

    def s1_kv_k(b2):
        par = b2 % 2
        def f(t):
            for dc in range(DC):
                lh = xcb[par][:, dc * 128:(dc + 1) * 128]
                s0 = dc == 0
                mm(t, PBF[2], lh, wk_i[:, dc * FK: dc * FK + FKH], s0, False)
                mm(t, PBF[3], lh, wk_i[:, dc * FK + FKH: dc * FK + FK], s0, False)
            mm(t, PBF[2], trow2p[par][:, :], bkr[0:1, 0:FKH], False, True)
            return mm(t, PBF[3], trow2p[par][:, :], bkr[0:1, FKH:FK], False, True)
        return f

    def s1_kv_v(b2):
        par = b2 % 2
        def f(t):
            for dc in range(DC):
                lh = xcb[par][:, dc * 128:(dc + 1) * 128]
                s0 = dc == 0
                mm(t, PBF[4], lh, wv_i[:, dc * FK: dc * FK + FKH], s0, False)
                mm(t, PBF[5], lh, wv_i[:, dc * FK + FKH: dc * FK + FK], s0, False)
            mm(t, PBF[4], trow2p[par][:, :], bvr[0:1, 0:FKH], False, True)
            return mm(t, PBF[5], trow2p[par][:, :], bvr[0:1, FKH:FK], False, True)
        return f

    def s1_xbc(t):
        return mm(t, PBF[1][:, 0:128], onesr[:, :], trow[:, :], True, True)

    def s1_xbev(e):
        return e.activation(negmB[:, :], PBF[1][:, 0:128], AF.Copy)

    def s1_xc(b2):
        par = b2 % 2
        nb = negmB[:, :].rearrange("p u -> p () u").to_broadcast((128, 32, 128))
        def f(e):
            return e.tensor_tensor(
                xcb[par][:, :].rearrange("p (c u) -> p c u", u=128),
                xph[par][:, :].rearrange("p (c u) -> p c u", u=128), nb, ALU.add)
        return f

    def s1_xcw(b2):
        par = b2 % 2
        hf, t0 = b2 // 8, (b2 % 8) * 128
        def f(e):
            ap = xcc[hf, :, :].rearrange("p (c t) -> p c t", c=DC)
            return e.dma_start(
                out=ap[:, :, t0:t0 + 128],
                in_=xcb[par][:, :].rearrange("p (c u) -> p c u", u=128))
        return f

    def s1_kev(e):
        e.activation(kr[:, :], PBF[2], AF.Relu, scale=invscol[:, :])
        e.activation(krn[:, :], PBF[2], AF.Relu, scale=ninvscol[:, :])
        return e.activation(ke[:, :], krn[:, :], AF.Exp, scale=-1.0)

    def s1_vev(e):
        e.activation(kvb_v[:, 0:FKH], PBF[4], AF.Copy, scale=invscol[:, :])
        return e.activation(kvb_v[:, FKH:FK], PBF[5], AF.Copy, scale=invscol[:, :])

    def s1_kev2(e):
        e.activation(kr[:, :], PBF[3], AF.Relu, scale=invscol[:, :])
        e.activation(krn[:, :], PBF[3], AF.Relu, scale=ninvscol[:, :])
        return e.activation(ke[:, :], krn[:, :], AF.Exp, scale=-1.0)

    def s1_kadd(hlf):
        return lambda e: e.tensor_tensor(kvb_k[:, hlf * FKH:(hlf + 1) * FKH],
                                         kr[:, :], ke[:, :], ALU.add)

    def s1_kvt(b2):
        def f(t):
            for h in range(HKV):
                bank = PBF[6] if h < 4 else PBF[7]
                col = (h % 4) * 128
                mm(t, bank[:, col:col + 128], kvb_k[:, h * 128:(h + 1) * 128],
                   kvb_v[:, h * 128:(h + 1) * 128],
                   b2 == 0 and h % 4 == 0, b2 == NBLK - 1)
            mm(t, PB[2][0:1, 0:512], onesc[:, :], kvb_k[:, 0:FKH], True, True)
            return mm(t, PB[3][0:1, 0:512], onesc[:, :], kvb_k[:, FKH:FK], True, True)
        return f

    def s1_ksacc(e):
        e.tensor_tensor(ksum[0:1, 0:FKH], ksum[0:1, 0:FKH], PB[2][0:1, 0:512],
                        ALU.add)
        return e.tensor_tensor(ksum[0:1, FKH:FK], ksum[0:1, FKH:FK],
                               PB[3][0:1, 0:512], ALU.add)

    tg.task("gp", s1_load(0), writes=["xph0"], tag="s1l0")
    tg.task("gp", s1_load(1), writes=["xph1"], tag="s1l1")
    for hlf in range(2):
        tg.task("dve", s1_sq(0, hlf), reads=["xph0"], writes=[f"sqb{hlf}"],
                tag=f"s1q0.{hlf}")
        tg.task("pe", s1_stats(0, hlf), reads=["xph0", f"sqb{hlf}", "onesc"],
                writes=["b0"], tag=f"s1t0.{hlf}")
    tg.task("act", s1_rows_a, reads=["b0"], writes=["frows", "trows"], tag="s1a0")
    tg.task("dve", s1_rows_b, reads=["frows"], writes=["frows"], tag="s1b0")
    tg.task("act", s1_rows_c(0), reads=["frows", "epsc"],
            writes=["trows", "trow2a"], tag="s1c0")
    tg.task("act", s1_rows_own(0), reads=["trows"], writes=["ownrows"], tag="s1o0")
    tg.task("pe", s1_xbc, reads=["trows", "onesr"], writes=["b1"], tag="s1xb0")
    tg.task("act", s1_xbev, reads=["b1"], writes=["negmB"], tag="s1xe0")
    tg.task("dve", s1_xc(0), reads=["xph0", "negmB"], writes=["xcb0"],
            tag="s1xc0")
    tg.task("sp", s1_xcw(0), reads=["xcb0"], writes=["xcc"], tag="s1xw0")
    for b2 in range(NBLK):
        nxt = (b2 + 1) % 2
        if b2 + 2 < NBLK:
            tg.task("gp", s1_load(b2 + 2), writes=[f"xph{b2 % 2}"],
                    tag=f"s1l{b2+2}")
        if b2 == 0:
            tg.task("pe", s1_invcol, reads=["trows", "onesr"], writes=["b0"],
                    tag="s1i0")
            tg.task("act", s1_invev, reads=["b0"], writes=["invc"], tag="s1ie0")
        if b2 + 1 < NBLK:
            for hlf in range(2):
                tg.task("dve", s1_sq(b2 + 1, hlf), reads=[f"xph{nxt}"],
                        writes=[f"sqb{hlf}"], tag=f"s1q{b2+1}.{hlf}")
                tg.task("pe", s1_stats(b2 + 1, hlf),
                        reads=[f"xph{nxt}", f"sqb{hlf}", "onesc"], writes=["b0"],
                        tag=f"s1t{b2+1}.{hlf}")
            tg.task("act", s1_rows_a, reads=["b0"], writes=["frows", "trows"],
                    tag=f"s1a{b2+1}")
            tg.task("dve", s1_rows_b, reads=["frows"], writes=["frows"],
                    tag=f"s1b{b2+1}")
            tg.task("act", s1_rows_c((b2 + 1) % 2), reads=["frows", "epsc"],
                    writes=["trows", f"trow2{'ab'[(b2+1) % 2]}"],
                    tag=f"s1c{b2+1}")
            if b2 + 1 < NOWNB:
                tg.task("act", s1_rows_own(b2 + 1), reads=["trows"],
                        writes=["ownrows"], tag=f"s1o{b2+1}")
        tg.task("pe", s1_kv_k(b2), reads=[f"xcb{b2 % 2}", "wk_i",
                                          f"trow2{'ab'[b2 % 2]}", "bkr"],
                writes=["b2", "b3"], tag=f"s1kk{b2}")
        tg.task("act", s1_kev, reads=["b2", "b3", "invc"], writes=["phis"],
                tag=f"s1ke{b2}")
        tg.task("dve", s1_kadd(0), reads=["phis"], writes=["kvb_k"], tag=f"s1ka{b2}")
        tg.task("act", s1_kev2, reads=["b3", "invc"], writes=["phis"],
                tag=f"s1k2{b2}")
        tg.task("dve", s1_kadd(1), reads=["phis"], writes=["kvb_k"], tag=f"s1kb{b2}")
        tg.task("pe", s1_kv_v(b2), reads=[f"xcb{b2 % 2}", "wv_i",
                                          f"trow2{'ab'[b2 % 2]}", "bvr"],
                writes=["b4", "b5"], tag=f"s1kv{b2}")
        tg.task("act", s1_vev, reads=["b4", "b5", "invc"], writes=["kvb_v"],
                tag=f"s1ve{b2}")
        if b2 + 1 < NBLK:
            tg.task("pe", s1_xbc, reads=["trows", "onesr"], writes=["b1"],
                    tag=f"s1xb{b2+1}")
            tg.task("act", s1_xbev, reads=["b1"], writes=["negmB"],
                    tag=f"s1xe{b2+1}")
            tg.task("pe", s1_invcol, reads=["trows", "onesr"], writes=["b0"],
                    tag=f"s1i{b2+1}")
            tg.task("act", s1_invev, reads=["b0"], writes=["invc"],
                    tag=f"s1ie{b2+1}")
            tg.task("dve", s1_xc(b2 + 1), reads=[f"xph{nxt}", "negmB"],
                    writes=[f"xcb{nxt}"], tag=f"s1xc{b2+1}")
            if b2 + 1 < NOWNB:
                tg.task("sp", s1_xcw(b2 + 1), reads=[f"xcb{nxt}"],
                        writes=["xcc"], tag=f"s1xw{b2+1}")
        tg.task("pe", s1_kvt(b2), reads=["kvb_k", "kvb_v", "onesc"],
                writes=["b6", "b7", "b2", "b3"], tag=f"s1v{b2}")
        tg.task("dve", s1_ksacc, reads=["b2", "b3"], writes=["ksum"], tag=f"s1s{b2}")

    tg.task("act", lambda e: (e.activation(kvt[:, 0:512], PBF[6], AF.Copy),
                              e.activation(kvt[:, 512:FK], PBF[7], AF.Copy))[-1],
            reads=["b6", "b7"], writes=["kvt"], tag="kvte")
    tg.task("act", lambda e: (e.activation(ke[0:1, 0:512], ksum[0:1, 0:FKH],
                                           AF.Copy, bias=EPS_ATT),
                              e.activation(krn[0:1, 0:512], ksum[0:1, FKH:FK],
                                           AF.Copy, bias=EPS_ATT))[-1],
            reads=["ksum"], writes=["phis"], tag="kse")
    tg.task("dve", lambda e: (e.reciprocal(kr[0:1, 0:512], ke[0:1, 0:512]),
                              e.reciprocal(ke[0:1, 0:512], krn[0:1, 0:512]))[-1],
            reads=["phis"], writes=["phis"], tag="ksr")
    tg.task("act", lambda e: (e.activation(reciprow[0:1, 0:FKH], kr[0:1, 0:512], AF.Copy),
                              e.activation(reciprow[0:1, FKH:FK], ke[0:1, 0:512],
                                           AF.Copy))[-1],
            reads=["phis"], writes=["reciprow"], tag="ksc2")

    def recip_cols(t):
        last = None
        for h in range(HKV):
            last = mm(t, PB[0][:, h:h + 1], reciprow[0:1, h * 128:(h + 1) * 128],
                      onesr[0:1, 0:1], h == 0, h == HKV - 1)
        return last
    tg.task("pe", recip_cols, reads=["reciprow", "onesr"], writes=["b0"], tag="rc")
    tg.task("act", lambda e: e.activation(recipc[:, :], PB[0][:, 0:HKV], AF.Copy),
            reads=["b0"], writes=["recipc"], tag="rce")

    if debug:
        tg.task("sp", lambda e: e.dma_start(out=dbg_kvt[:, :], in_=kvt[:, :]),
                reads=["kvt"], writes=["dbg"], tag="dkvt")
        tg.task("sp", lambda e: e.dma_start(out=dbg_rows[0:1, OWN:2 * OWN],
                                            in_=own_invs[:, :]),
                reads=["ownrows"], writes=["dbg"], tag="drow2")
        tg.task("sp", lambda e: e.dma_start(out=dbg_recip[:, :], in_=reciprow),
                reads=["reciprow"], writes=["dbg"], tag="drec")
    tg.barrier()

    al.reset()
    xpo = al.take("xpo", DC * P2H)
    ATTN = al.take("ATTN", DC * P2H)
    wqb = [al.take(f"wqb{i}", D) for i in range(2)]
    epi_base = al.mark()
    qh = [al.take(f"qh{i}", P2H) for i in range(2)]
    u_s = al.take("u_s", 2 * P2H, F32)
    r_s = al.take("r_s", 2 * P2H, F32)
    rn_s = al.take("rn_s", 2 * P2H, F32)
    invsb = al.take("invsb", 2 * P2H, F32)
    stat_end = al.mark()
    browR = al.take("browR", P2H, parts=1)
    browC = al.take("browC", P2H, parts=1)
    frA_off = al.mark()
    frA = al.take("frA", 2 * P2H, F32, parts=1)
    frB = al.take("frB", 2 * P2H, F32, parts=1)
    al.gen += 1
    al.seek(frA_off)
    pls = [al.take(f"pls{i}", 2 * P2H, F32) for i in range(2)]
    al.gen -= 1
    tail_base = al.mark()
    al.gen += 1
    al.seek(epi_base)
    xres = [al.take(f"xres{i}", 2 * P2H, F32) for i in range(2)]
    x2b = [al.take(f"x2b{i}", 2 * P2H, F32) for i in range(2)]
    sqc = [al.take(f"sqc{i}", P2H) for i in range(2)]
    assert al.mark() <= stat_end
    al.seek(tail_base)
    xh2w = [al.take(f"xh2w{i}", P2H) for i in range(2)]
    rb_b = al.take("rb_b", 2 * P2H, F32)
    cb_b = al.take("cb_b", 2 * P2H, F32)
    assert al.top <= ARENA_N

    def p2_rows(tag):
        def a(e):
            e.activation(frA[0:1, 0:512], PB[6][0:1, 0:512], AF.Copy, scale=1.0 / D)
            e.activation(frA[0:1, 512:P2H], PB[7][0:1, 0:512], AF.Copy,
                         scale=1.0 / D)
            e.activation(frB[0:1, 0:512], PB[4][0:1, 0:512], AF.Copy, scale=1.0 / D)
            return e.activation(frB[0:1, 512:P2H], PB[5][0:1, 0:512], AF.Copy,
                                scale=1.0 / D)

        def b(e):
            e.tensor_tensor(PB[4][0:1, 0:512], frA[0:1, 0:512], frA[0:1, 0:512],
                            ALU.mult)
            e.tensor_tensor(PB[5][0:1, 0:512], frA[0:1, 512:P2H],
                            frA[0:1, 512:P2H], ALU.mult)
            e.tensor_tensor(PB[6][0:1, 0:512], frB[0:1, 0:512], PB[4][0:1, 0:512],
                            ALU.subtract)
            return e.tensor_tensor(PB[7][0:1, 0:512], frB[0:1, 512:P2H],
                                   PB[5][0:1, 0:512], ALU.subtract)

        def c(e):
            e.activation(PB[2][0:1, 0:512], PB[6][0:1, 0:512], AF.Ln, bias=epsln[:, :])
            e.activation(PB[3][0:1, 0:512], PB[7][0:1, 0:512], AF.Ln, bias=epsln[:, :])
            e.activation(PB[4][0:1, 0:512], PB[2][0:1, 0:512], AF.Exp, scale=-0.5)
            e.activation(PB[5][0:1, 0:512], PB[3][0:1, 0:512], AF.Exp, scale=-0.5)
            e.activation(browR[0:1, 0:512], PB[4][0:1, 0:512], AF.Copy)
            return e.activation(browR[0:1, 512:P2H], PB[5][0:1, 0:512], AF.Copy)

        def d(e):
            e.tensor_tensor(PB[6][0:1, 0:512], frA[0:1, 0:512], PB[4][0:1, 0:512],
                            ALU.mult)
            return e.tensor_tensor(PB[7][0:1, 0:512], frA[0:1, 512:P2H],
                                   PB[5][0:1, 0:512], ALU.mult)

        def d2(e):
            e.activation(browC[0:1, 0:512], PB[6][0:1, 0:512], AF.Copy, scale=-1.0)
            return e.activation(browC[0:1, 512:P2H], PB[7][0:1, 0:512], AF.Copy,
                                scale=-1.0)

        tg.task("act", a, reads=["b4", "b5", "b6", "b7"], writes=["frAB"],
                tag=tag + "a")
        tg.task("dve", b, reads=["frAB"], writes=["b4", "b5", "b6", "b7"],
                tag=tag + "b")
        tg.task("act", c, reads=["b6", "b7", "epsc"],
                writes=["b2", "b3", "b4", "b5", "browR"], tag=tag + "c")
        tg.task("dve", d, reads=["frAB", "b4", "b5"], writes=["b6", "b7"],
                tag=tag + "d")
        tg.task("act", d2, reads=["b6", "b7"], writes=["browC"], tag=tag + "e")

    def p2_bcast(tag):
        def f(t):
            mm(t, PBF[6], onesr[:, :], browR[0:1, 0:512], True, True)
            mm(t, PBF[7], onesr[:, :], browR[0:1, 512:P2H], True, True)
            mm(t, PBF[4], onesr[:, :], browC[0:1, 0:512], True, True)
            return mm(t, PBF[5], onesr[:, :], browC[0:1, 512:P2H], True, True)

        def ev(e):
            e.activation(rb_b[:, 0:512], PBF[6], AF.Copy)
            e.activation(rb_b[:, 512:P2H], PBF[7], AF.Copy)
            e.activation(cb_b[:, 0:512], PBF[4], AF.Copy)
            return e.activation(cb_b[:, 512:P2H], PBF[5], AF.Copy)

        tg.task("pe", f, reads=["browR", "browC", "onesr"],
                writes=["b4", "b5", "b6", "b7"], tag=tag + "m")
        tg.task("act", ev, reads=["b4", "b5", "b6", "b7"], writes=["rbcb"],
                tag=tag + "v")

    qpair = [((PBF[0], "b0"), (PBF[1], "b1")), ((PBF[2], "b2"), (PBF[3], "b3"))]
    _f1w_pre = [arena_t[0:128, off:off + D]
                for off in (65536, 69632, 73728, 77824)]

    def f1_prefetch():
        tg.seed("gwb0", ["wqb0"])
        tg.seed("gwb1", ["wqb1"])
        tg.task("gp", lambda e: e.dma_start(out=_f1w_pre[0], in_=gw_t[0, :, :]),
                reads=["wqb0"], writes=["gwb0"], tag="f1pg0")
        tg.task("gp", lambda e: e.dma_start(out=_f1w_pre[1], in_=gw_t[1, :, :]),
                reads=["wqb1"], writes=["gwb1"], tag="f1pg1")

    def p2_half(hf):
        hoff = hf * P2H
        ohat = xpo
        xh2pan = ATTN
        if hf == 1:
            for d in range(DC):
                tg.seed(f"atc{d}", [f"xh2p{d}"])
            tg.seed("qh0", ["xres0"])
            tg.seed("qh1", ["xres0"])
            tg.seed("u_s", ["xres1"])
            tg.seed("r_s", ["x2b0"])
            tg.seed("rn_s", ["x2b1"])
            tg.seed("invsb", ["sqc0", "sqc1"])

        if hf == 0:
            for dc in range(DC):
                tg.seed(f"xpo{dc}", ["xpo"])
                tg.task("sp", lambda e, dc=dc: e.dma_start(
                            out=xpo[:, dc * P2H:(dc + 1) * P2H],
                            in_=xcc[0, :, dc * P2H:(dc + 1) * P2H]),
                        reads=["xcc"], writes=[f"xpo{dc}"], tag=f"p2x0.{dc}")

        def invsb_mm(t):
            mm(t, PBF[6], onesr[:, :], own_invs[0:1, hoff:hoff + 512], True, True)
            return mm(t, PBF[7], onesr[:, :], own_invs[0:1, hoff + 512:hoff + P2H],
                      True, True)
        tg.task("pe", invsb_mm, reads=["ownrows", "onesr"], writes=["b6", "b7"],
                tag=f"p2i{hf}")
        tg.task("act", lambda e: (e.activation(invsb[:, 0:512], PBF[6], AF.Copy),
                                  e.activation(invsb[:, 512:P2H], PBF[7],
                                               AF.Copy))[-1],
                reads=["b6", "b7"], writes=["invsb"], tag=f"p2ie{hf}")

        def wq_load(fc):
            return lambda e: e.dma_start(out=wqb[fc % 2], in_=wq_t[fc, :, :])

        def q_mm_h(fc, hlf):
            (pa, _), (pb_, _) = qpair[fc % 2]
            def f(t):
                last = None
                for dc in range(hlf * 16, hlf * 16 + 16):
                    lh = wqb[fc % 2][:, dc * 128:(dc + 1) * 128]
                    mm(t, pa, lh, xpo[:, dc * P2H: dc * P2H + 512], dc == 0,
                       dc == DC - 1)
                    last = mm(t, pb_, lh, xpo[:, dc * P2H + 512:(dc + 1) * P2H],
                              dc == 0, dc == DC - 1)
                return last
            return f

        def q_u(fc):
            (pa, _), (pb_, _) = qpair[fc % 2]
            def f(e):
                e.tensor_tensor(u_s[:, 0:512], pa, invsb[:, 0:512], ALU.mult)
                return e.tensor_tensor(u_s[:, 512:P2H], pb_, invsb[:, 512:P2H],
                                       ALU.mult)
            return f

        def q_phi_a(fc):
            def f(e):
                e.activation(r_s, u_s, AF.Relu, bias=bqc[:, fc:fc + 1])
                e.activation(rn_s, u_s, AF.Relu, scale=-1.0, bias=bqnc[:, fc:fc + 1])
                return e.activation(u_s, rn_s, AF.Exp, scale=-1.0)
            return f

        def q_phi_b(fc):
            return lambda e: e.tensor_tensor(qh[fc % 2], r_s, u_s, ALU.add)

        def at_mm(fc):
            kvh = fc // 4
            def f(t):
                lh = kvt[:, kvh * 128:(kvh + 1) * 128]
                mm(t, PBF[4], lh, qh[fc % 2][:, 0:512], True, True)
                return mm(t, PBF[5], lh, qh[fc % 2][:, 512:P2H], True, True)
            return f

        def at_ev(fc):
            kvh = fc // 4
            def f(e):
                e.activation(ATTN[:, fc * P2H: fc * P2H + 512], PBF[4], AF.Copy,
                             scale=recipc[:, kvh:kvh + 1])
                return e.activation(ATTN[:, fc * P2H + 512:(fc + 1) * P2H], PBF[5],
                                    AF.Copy, scale=recipc[:, kvh:kvh + 1])
            return f

        def q_evac(fc):
            (_, na), (_, nb) = qpair[fc % 2]
            tg.task("dve", q_u(fc), reads=[na, nb, "invsb"], writes=["u_s"],
                    tag=f"pu{fc}.{hf}")
            tg.task("act", q_phi_a(fc), reads=["u_s", "bqc", "bqnc"],
                    writes=["r_s", "rn_s", "u_s"], tag=f"pp{fc}.{hf}")
            tg.task("dve", q_phi_b(fc), reads=["r_s", "u_s"],
                    writes=[f"qh{fc % 2}"], tag=f"pf{fc}.{hf}")

        def at_emit(fc):
            tg.task("pe", at_mm(fc), reads=["kvt", f"qh{fc % 2}"],
                    writes=["b4", "b5"], tag=f"pa{fc}.{hf}")
            tg.task("act", at_ev(fc), reads=["b4", "b5", "recipc"],
                    writes=["ATTN", f"atc{fc}"], tag=f"pe{fc}.{hf}")

        if hf == 0:
            tg.task("gp", wq_load(0), writes=["wqb0"], tag=f"pw0.{hf}")
            tg.task("gp", wq_load(1), writes=["wqb1"], tag=f"pw1.{hf}")
        for fc in range(DC):
            (_, na), (_, nb) = qpair[fc % 2]
            tg.task("pe", q_mm_h(fc, 0),
                    reads=[f"wqb{fc % 2}"] + [f"xpo{d}" for d in range(16)],
                    writes=[na, nb], tag=f"pqa{fc}.{hf}")
            tg.task("pe", q_mm_h(fc, 1),
                    reads=[f"wqb{fc % 2}"] + [f"xpo{d}" for d in range(16, DC)],
                    writes=[na, nb], tag=f"pqb{fc}.{hf}")
            if fc + 2 < DC:
                tg.task("gp", wq_load(fc + 2), writes=[f"wqb{fc % 2}"],
                        tag=f"pw{fc+2}.{hf}")
            if fc > 0:
                at_emit(fc - 1)
            q_evac(fc)
        at_emit(DC - 1)

        for dc in range(DC):
            tg.seed(f"oh{dc}", [f"xpo{dc}"])

        def wo_load(fc):
            return lambda e: e.dma_start(out=wqb[fc % 2], in_=wo_t[fc, :, :])

        def o_mm(fc):
            (pa, _), (pb_, _) = qpair[fc % 2]
            def f(t):
                last = None
                for cc in range(DC):
                    lh = wqb[fc % 2][:, cc * 128:(cc + 1) * 128]
                    mm(t, pa, lh, ATTN[:, cc * P2H: cc * P2H + 512], cc == 0, False)
                    last = mm(t, pb_, lh, ATTN[:, cc * P2H + 512:(cc + 1) * P2H],
                              cc == 0, cc == DC - 1)
                return last
            return f

        def o_ev(fc):
            (pa, _), (pb_, _) = qpair[fc % 2]
            def f(e):
                e.activation(ohat[:, fc * P2H: fc * P2H + 512], pa, AF.Identity,
                             bias=boc[:, fc:fc + 1])
                return e.activation(ohat[:, fc * P2H + 512:(fc + 1) * P2H], pb_,
                                    AF.Identity, bias=boc[:, fc:fc + 1])
            return f

        def p_sq(idx, pan):
            def f(e):
                sl = slice(idx * P2H, (idx + 1) * P2H)
                return e.tensor_tensor(sqc[idx % 2], pan[:, sl], pan[:, sl],
                                       ALU.mult)
            return f

        def p_stat(idx, pan):
            def f(t):
                sl = slice(idx * P2H, idx * P2H + 512)
                s2 = slice(idx * P2H + 512, (idx + 1) * P2H)
                mm(t, PB[6][0:1, 0:512], onesc[:, :], pan[:, sl], idx == 0, False)
                mm(t, PB[7][0:1, 0:512], onesc[:, :], pan[:, s2], idx == 0, False)
                mm(t, PB[4][0:1, 0:512], onesc[:, :], sqc[idx % 2][:, 0:512],
                   idx == 0, False)
                return mm(t, PB[5][0:1, 0:512], onesc[:, :],
                          sqc[idx % 2][:, 512:P2H], idx == 0, idx == DC - 1)
            return f

        tg.task("gp", wo_load(0), writes=["wqb0"], tag=f"po0.{hf}")
        for fc in range(DC):
            (_, na), (_, nb) = qpair[fc % 2]
            if fc + 1 < DC:
                tg.task("gp", wo_load(fc + 1), writes=[f"wqb{(fc+1) % 2}"],
                        tag=f"po{fc+1}.{hf}")
            tg.task("pe", o_mm(fc), reads=[f"wqb{fc % 2}", "ATTN"],
                    writes=[na, nb], tag=f"pm{fc}.{hf}")
            tg.task("act", o_ev(fc), reads=[na, nb, "boc"], writes=[f"oh{fc}"],
                    tag=f"pv{fc}.{hf}")
            tg.task("dve", p_sq(fc, ohat), reads=[f"oh{fc}"],
                    writes=[f"sqc{fc % 2}"], tag=f"ps{fc}.{hf}")
            if fc > 0:
                tg.task("pe", p_stat(fc - 1, ohat),
                        reads=[f"oh{fc-1}", f"sqc{(fc-1) % 2}", "onesc"],
                        writes=["b4", "b5", "b6", "b7"], tag=f"pz{fc-1}.{hf}")
        tg.task("pe", p_stat(DC - 1, ohat),
                reads=[f"oh{DC-1}", f"sqc{(DC-1) % 2}", "onesc"],
                writes=["b4", "b5", "b6", "b7"], tag=f"pz{DC-1}.{hf}")
        if hf == 0:
            tg.task("gp", wq_load(0), writes=["wqb0"], tag="pw0.1p")
            tg.task("gp", wq_load(1), writes=["wqb1"], tag="pw1.1p")

        if hf == 1:
            f1_prefetch()
        p2_rows(f"lo{hf}")
        p2_bcast(f"lb{hf}")
        if debug and hf == 0:
            tg.task("sp", lambda e: e.dma_start(out=dbg_attn[:, :], in_=ATTN),
                    reads=["ATTN"], writes=["dbg"], tag="dattn")
            tg.task("sp", lambda e: e.dma_start(out=dbg_oh[:, :], in_=ohat),
                    reads=[f"oh{d}" for d in range(DC)], writes=["dbg"], tag="doh")
            tg.task("sp", lambda e: e.dma_start(out=dbg_qh[:, :], in_=qh[(DC - 1) % 2]),
                    reads=[f"qh{(DC-1) % 2}"], writes=["dbg"], tag="dqh")
        tg.barrier()

        for d in range(DC):
            tg.seed(f"xh2p{d}", ["ATTN"])

        def ep1_xload(dc):
            return lambda e: e.dma_start(out=xres[dc % 2],
                                         in_=xp2[hf, :, dc * P2H:(dc + 1) * P2H])

        def ep1_y1(dc):
            def f(e):
                sl = slice(dc * P2H, (dc + 1) * P2H)
                e.tensor_tensor(PBF[0], ohat[:, sl][:, 0:512], rb_b[:, 0:512],
                                ALU.mult)
                e.tensor_tensor(PBF[1], ohat[:, sl][:, 512:P2H], rb_b[:, 512:P2H],
                                ALU.mult)
                e.tensor_tensor(PBF[2], PBF[0], cb_b[:, 0:512], ALU.add)
                e.tensor_tensor(PBF[3], PBF[1], cb_b[:, 512:P2H], ALU.add)
                e.tensor_tensor(x2b[dc % 2][:, 0:512], PBF[2],
                                xres[dc % 2][:, 0:512], ALU.add)
                return e.tensor_tensor(x2b[dc % 2][:, 512:P2H], PBF[3],
                                       xres[dc % 2][:, 512:P2H], ALU.add)
            return f

        def ep1_cast(dc):
            return lambda e: e.activation(xh2pan[:, dc * P2H:(dc + 1) * P2H],
                                          x2b[dc % 2], AF.Copy)

        def ep1_x2w(dc):
            return lambda e: e.dma_start(out=x2c[dc, :, hf * P2H:(hf + 1) * P2H],
                                         in_=x2b[dc % 2])

        def ep1_y1p(dc):
            def f(e):
                sl = slice(dc * P2H, (dc + 1) * P2H)
                e.tensor_tensor(pls[0], ohat[:, sl], rb_b, ALU.mult)
                e.tensor_tensor(pls[1], pls[0], cb_b, ALU.add)
                return e.tensor_tensor(x2b[dc % 2], pls[1], xres[dc % 2], ALU.add)
            return f

        tg.seed("pls", ["frAB"])
        for dc in range(DC):
            tg.task("gp", ep1_xload(dc), writes=[f"xres{dc % 2}"],
                    tag=f"ex{dc}.{hf}")
            tg.task("dve", ep1_y1p(dc), reads=[f"oh{dc}", "rbcb",
                                               f"xres{dc % 2}"],
                    writes=[f"x2b{dc % 2}", "pls"], tag=f"ey{dc}.{hf}")
            if hf == 0:
                tg.task("sp", lambda e, dc=dc: e.dma_start(
                            out=xpo[:, dc * P2H:(dc + 1) * P2H],
                            in_=xcc[1, :, dc * P2H:(dc + 1) * P2H]),
                        reads=["xcc"], writes=[f"xpo{dc}", f"oh{dc}"],
                        tag=f"p2x1.{dc}")
            tg.task("act", ep1_cast(dc), reads=[f"x2b{dc % 2}"],
                    writes=[f"xh2p{dc}"], tag=f"ec{dc}.{hf}")
            if dc > 0:
                tg.task("dve", p_sq(dc - 1, xh2pan), reads=[f"xh2p{dc-1}"],
                        writes=[f"sqc{(dc-1) % 2}"], tag=f"es{dc-1}.{hf}")
            if dc > 1:
                tg.task("pe", p_stat(dc - 2, xh2pan),
                        reads=[f"xh2p{dc-2}", f"sqc{dc % 2}", "onesc"],
                        writes=["b4", "b5", "b6", "b7"], tag=f"ez{dc-2}.{hf}")
            tg.task("sp", ep1_x2w(dc), reads=[f"x2b{dc % 2}"], writes=["x2c"],
                    tag=f"ew{dc}.{hf}")
        tg.task("dve", p_sq(DC - 1, xh2pan), reads=[f"xh2p{DC-1}"],
                writes=[f"sqc{(DC-1) % 2}"], tag=f"es{DC-1}.{hf}")
        tg.task("pe", p_stat(DC - 2, xh2pan),
                reads=[f"xh2p{DC-2}", f"sqc{(DC-2) % 2}", "onesc"],
                writes=["b4", "b5", "b6", "b7"], tag=f"ez{DC-2}.{hf}")
        tg.task("pe", p_stat(DC - 1, xh2pan),
                reads=[f"xh2p{DC-1}", f"sqc{(DC-1) % 2}", "onesc"],
                writes=["b4", "b5", "b6", "b7"], tag=f"ez{DC-1}.{hf}")

        p2_rows(f"l2{hf}")
        p2_bcast(f"l2b{hf}")
        tg.barrier()

        def ep2_xh2(dc):
            def f(e):
                sl = slice(dc * P2H, (dc + 1) * P2H)
                e.tensor_tensor(pls[dc % 2], xh2pan[:, sl], rb_b, ALU.mult)
                return e.tensor_tensor(xh2w[dc % 2], pls[dc % 2], cb_b, ALU.add)
            return f

        def ep2_w(dc):
            return lambda e: e.dma_start(out=xh2c[dc, :, hf * P2H:(hf + 1) * P2H],
                                         in_=xh2w[dc % 2])

        for dc in range(DC):
            tg.task("dve", ep2_xh2(dc), reads=[f"xh2p{dc}", "rbcb"],
                    writes=[f"xh2w{dc % 2}", "pls"], tag=f"eh{dc}.{hf}")
            tg.task("sp", ep2_w(dc), reads=[f"xh2w{dc % 2}"],
                    writes=[f"xh2c{dc}"], tag=f"ev{dc}.{hf}")

    p2_half(0)
    p2_half(1)

    al.reset()
    f1pan = al.take("f1pan", DC * OWN)
    gwb = [al.take(f"gwb{i}", D) for i in range(2)]
    uwb = [al.take(f"uwb{i}", D) for i in range(2)]
    silug = al.take("silug", OWN)
    upv = al.take("upv", OWN)
    hbuf = [al.take(f"hbuf{i}", OWN) for i in range(2)]
    f0b = [al.take(f"f0b{i}", 1376) for i in range(2)]
    assert al.top <= ARENA_N
    f0_jobs = [(fc, q) for fc in range(DC) for q in range(8)]
    f0z = [0]

    def f0_step():
        i = f0z[0]
        if i >= len(f0_jobs):
            return
        f0z[0] += 1
        fc, q = f0_jobs[i]
        tg.task("gp", lambda e: e.dma_start(out=f0b[i % 2],
                                            in_=dw_t[fc, :, q * 1376:(q + 1) * 1376]),
                writes=[f"f0b{i % 2}"], tag=f"f0l{i}")
        tg.task("sp", lambda e: e.dma_start(out=dwc[fc, :, q * 1376:(q + 1) * 1376],
                                            in_=f0b[i % 2]),
                reads=[f"f0b{i % 2}"], writes=["dwc"], tag=f"f0s{i}")

    def f1_wload(which, fc):
        src = gw_t if which == 0 else uw_t
        dst = gwb if which == 0 else uwb
        return lambda e: e.dma_start(out=dst[fc % 2], in_=src[fc, :, :])

    def f1_mm(which, fc):
        wsrc = gwb if which == 0 else uwb
        banks = PBF[0:4] if which == 0 else PBF[4:8]
        def f(t):
            last = None
            for dc in range(DC):
                lh = wsrc[fc % 2][:, dc * 128:(dc + 1) * 128]
                for sub in range(4):
                    last = mm(t, banks[sub], lh,
                              f1pan[:, dc * OWN + sub * 512: dc * OWN + (sub + 1) * 512],
                              dc == 0, dc == DC - 1)
            return last
        return f

    def f1_mm_dc(which, fc, dc):
        wsrc = gwb if which == 0 else uwb
        banks = PBF[0:4] if which == 0 else PBF[4:8]
        def f(t):
            last = None
            lh = wsrc[fc % 2][:, dc * 128:(dc + 1) * 128]
            for sub in range(4):
                last = mm(t, banks[sub], lh,
                          f1pan[:, dc * OWN + sub * 512: dc * OWN + (sub + 1) * 512],
                          dc == 0, dc == DC - 1)
            return last
        return f

    def f1_silu(fc):
        def f(e):
            last = None
            for sub in range(4):
                last = e.activation(silug[:, sub * 512:(sub + 1) * 512], PBF[sub],
                                    AF.Silu, scale=gsc[:, fc:fc + 1])
            return last
        return f

    def f1_up(fc):
        def f(e):
            last = None
            for sub in range(4):
                last = e.activation(upv[:, sub * 512:(sub + 1) * 512], PBF[4 + sub],
                                    AF.Copy, scale=usc[:, fc:fc + 1])
            return last
        return f

    tg.task("gp", f1_wload(1, 0), writes=["uwb0"], tag="f1u0")
    tg.task("gp", f1_wload(1, 1), writes=["uwb1"], tag="f1u1")
    for dc in range(DC):
        if dc < 16:
            war = [f"xpo{2*dc}", f"xpo{2*dc+1}", f"oh{2*dc}", f"oh{2*dc+1}"]
        else:
            war = [f"xh2p{2*(dc-16)}", f"xh2p{2*(dc-16)+1}"]
        tg.task("gp", lambda e, dc=dc: e.dma_start(
            out=f1pan[:, dc * OWN:(dc + 1) * OWN], in_=xh2c[dc, :, :]),
            reads=[f"xh2c{dc}"], writes=["f1pan", f"f1pan{dc}"] + war,
            tag=f"f1ld{dc}")
    if debug:
        tg.task("sp", lambda e: e.dma_start(out=dbg_gw[:, 0:D], in_=gwb[0][:, :]),
                reads=["gwb0"], writes=["dbg"], tag="dgw0")
        tg.task("sp", lambda e: e.dma_start(out=dbg_gw[:, D:2 * D], in_=gwb[1][:, :]),
                reads=["gwb1"], writes=["dbg"], tag="dgw1")
    for fc in range(FFC):
        if 1 < fc + 1 < FFC:
            tg.task("gp", f1_wload(0, fc + 1), writes=[f"gwb{(fc+1) % 2}"],
                    tag=f"f1g{fc+1}")
        if fc + 1 < FFC and fc + 1 > 1:
            tg.task("gp", f1_wload(1, fc + 1), writes=[f"uwb{(fc+1) % 2}"],
                    tag=f"f1u{fc+1}")
        if fc == 0:
            for dc in range(DC):
                tg.task("pe", f1_mm_dc(0, 0, dc), reads=["gwb0", f"f1pan{dc}"],
                        writes=["b0", "b1", "b2", "b3"], tag=f"f1m0.{dc}")
                tg.task("pe", f1_mm_dc(1, 0, dc), reads=["uwb0", f"f1pan{dc}"],
                        writes=["b4", "b5", "b6", "b7"], tag=f"f1n0.{dc}")
            tg.task("act", f1_silu(0), reads=["b0", "b1", "b2", "b3", "gsc"],
                    writes=["silug"], tag="f1s0")
            tg.task("act", f1_up(0), reads=["b4", "b5", "b6", "b7", "usc"],
                    writes=["upv"], tag="f1e0")
        else:
            tg.task("pe", f1_mm(0, fc), reads=[f"gwb{fc % 2}", "f1pan"],
                    writes=["b0", "b1", "b2", "b3"], tag=f"f1m{fc}")
            tg.task("act", f1_silu(fc), reads=["b0", "b1", "b2", "b3", "gsc"],
                    writes=["silug"], tag=f"f1s{fc}")
            tg.task("pe", f1_mm(1, fc), reads=[f"uwb{fc % 2}", "f1pan"],
                    writes=["b4", "b5", "b6", "b7"], tag=f"f1n{fc}")
            tg.task("act", f1_up(fc), reads=["b4", "b5", "b6", "b7", "usc"],
                    writes=["upv"], tag=f"f1e{fc}")
        tg.task("dve", lambda e, fc=fc: e.tensor_tensor(hbuf[fc % 2], silug, upv,
                                                        ALU.mult),
                reads=["silug", "upv"], writes=[f"hbuf{fc % 2}"], tag=f"f1h{fc}")
        tg.task("sp", lambda e, fc=fc: e.dma_start(out=hc[fc, :, :],
                                                   in_=hbuf[fc % 2]),
                reads=[f"hbuf{fc % 2}"], writes=["hc"], tag=f"f1w{fc}")
        f0_step(); f0_step(); f0_step()

    al.reset()
    htile = al.take("htile", FFC * F2T)
    dwb = [al.take(f"dwb{i}", 5504) for i in range(3)]
    dpan = al.take("dpan", DC * F2T)
    x2t = [al.take(f"x2t{i}", 2 * F2T, F32) for i in range(2)]
    outb = [al.take(f"outb{i}", 2 * F2T, F32) for i in range(2)]
    sqf = [al.take(f"sqf{i}", F2T) for i in range(2)]
    tsf = [al.take(f"tsf{i}", 2 * F2T, F32) for i in range(2)]
    brR = al.take("brR", 512, parts=1)
    brC = al.take("brC", 512, parts=1)
    frC = al.take("frC", 1024, F32, parts=1)
    frD = al.take("frD", 1024, F32, parts=1)
    rbS = al.take("rbS", 1024, F32)
    csb = al.take("csb", 1024, F32)
    assert al.top <= ARENA_N

    def f2_stat(t, fc):
        mm(t, PB[2][0:1, 0:512], onesc[:, :], dpan[:, fc * F2T:(fc + 1) * F2T],
           fc == 0, False)
        return mm(t, PB[3][0:1, 0:512], onesc[:, :], sqf[fc % 2],
                  fc == 0, fc == DC - 1)

    def f2_rows(tag):
        def a(e):
            e.activation(frC[0:1, 0:512], PB[2][0:1, 0:512], AF.Copy, scale=1.0 / D)
            return e.activation(frD[0:1, 0:512], PB[3][0:1, 0:512], AF.Copy,
                                scale=1.0 / D)

        def b(e):
            e.tensor_tensor(PB[6][0:1, 0:512], frC[0:1, 0:512], frC[0:1, 0:512],
                            ALU.mult)
            return e.tensor_tensor(PB[7][0:1, 0:512], frD[0:1, 0:512],
                                   PB[6][0:1, 0:512], ALU.subtract)

        def c(e):
            e.activation(PB[5][0:1, 0:512], PB[7][0:1, 0:512], AF.Ln, bias=epsln[:, :])
            e.activation(PB[6][0:1, 0:512], PB[5][0:1, 0:512], AF.Exp, scale=-0.5)
            return e.activation(brR[0:1, 0:512], PB[6][0:1, 0:512], AF.Copy)

        def d(e):
            return e.tensor_tensor(PB[7][0:1, 0:512], frC[0:1, 0:512],
                                   PB[6][0:1, 0:512], ALU.mult)

        def d2(e):
            return e.activation(brC[0:1, 0:512], PB[7][0:1, 0:512], AF.Copy,
                                scale=-1.0)

        def bc(t):
            mm(t, PBF[2], onesr[:, :], brR[0:1, 0:512], True, True)
            return mm(t, PBF[3], onesr[:, :], brC[0:1, 0:512], True, True)

        tg.task("act", a, reads=["b2", "b3"], writes=["frCD"], tag=tag + "a")
        tg.task("dve", b, reads=["frCD"], writes=["b6", "b7"], tag=tag + "b")
        tg.task("act", c, reads=["b7", "epsc"], writes=["b5", "b6", "brR"], tag=tag + "c")
        tg.task("dve", d, reads=["frCD", "b6"], writes=["b7"], tag=tag + "d")
        tg.task("act", d2, reads=["b7"], writes=["brC"], tag=tag + "e")
        tg.task("pe", bc, reads=["brR", "brC", "onesr"], writes=["b2", "b3"],
                tag=tag + "f")

        def bcev(e):
            e.activation(rbS[:, :], PBF[2], AF.Copy)
            return e.activation(csb[:, :], PBF[3], AF.Copy)
        tg.task("act", bcev, reads=["b2", "b3"], writes=["rbcs"], tag=tag + "g")

    def f2_epi_step(tt, dc, q="gp"):
        tg.task(q, lambda e, dc=dc, tt=tt: e.dma_start(
                    out=x2t[dc % 2], in_=x2c[dc, :, tt * F2T:(tt + 1) * F2T]),
                reads=["x2c"], writes=[f"x2t{dc % 2}"], tag=f"f2x{dc}.{tt}")

        def f2_y2(e, dc=dc):
            sl = slice(dc * F2T, (dc + 1) * F2T)
            e.tensor_tensor(tsf[0], dpan[:, sl], rbS[:, :], ALU.mult)
            e.tensor_tensor(tsf[1], tsf[0], csb[:, :], ALU.add)
            return e.tensor_tensor(outb[dc % 2], tsf[1], x2t[dc % 2], ALU.add)
        tg.task("dve", f2_y2, reads=[f"dp{dc}", "rbcs", f"x2t{dc % 2}"],
                writes=[f"outb{dc % 2}", "tsf0", "tsf1"], tag=f"f2y{dc}.{tt}")
        tg.task(q, lambda e, dc=dc, tt=tt: e.dma_start(
                    out=outp[tt, :, dc * F2T:(dc + 1) * F2T], in_=outb[dc % 2]),
                reads=[f"outb{dc % 2}"], writes=["outp"], tag=f"f2o{dc}.{tt}")

    for tt in range(NT2):
        tg.task("sp", lambda e, tt=tt: e.dma_start(
            out=htile[:, :].rearrange("p (c t) -> p c t", c=FFC),
            in_=hc[:, :, tt * F2T:(tt + 1) * F2T].rearrange("c p t -> p c t")),
            reads=["hc"], writes=["htile"], tag=f"f2h{tt}")
        for hh in range(2):
            tg.task("sp", lambda e, hh=hh: e.dma_start(
                        out=dwb[hh], in_=dwc[0, :, hh * 5504:(hh + 1) * 5504]),
                    reads=["dwc"], writes=[f"dwb{hh}"], tag=f"f2w0{hh}.{tt}")
        for fc in range(DC):
            bname = f"b{fc % 2}"
            h0, h1, h2 = (2 * fc) % 3, (2 * fc + 1) % 3, (2 * fc + 2) % 3
            if tt > 0:
                f2_epi_step(tt - 1, fc)
            if fc + 1 < DC:
                tg.task("sp", lambda e, fc=fc, h2=h2: e.dma_start(
                            out=dwb[h2], in_=dwc[fc + 1, :, 0:5504]),
                        reads=["dwc"], writes=[f"dwb{h2}"], tag=f"f2wa{fc+1}.{tt}")

            def f2_mm_a(t, fc=fc, h0=h0):
                last = None
                for c in range(43):
                    last = mm(t, PBF[fc % 2], dwb[h0][:, c * 128:(c + 1) * 128],
                              htile[:, c * F2T:(c + 1) * F2T], c == 0, False)
                return last

            def f2_mm_b(t, fc=fc, h1=h1):
                last = None
                for c in range(43, FFC):
                    last = mm(t, PBF[fc % 2],
                              dwb[h1][:, (c - 43) * 128:(c - 42) * 128],
                              htile[:, c * F2T:(c + 1) * F2T], False, c == FFC - 1)
                return last
            tg.task("pe", f2_mm_a, reads=[f"dwb{h0}", "htile"], writes=[bname],
                    tag=f"f2ma{fc}.{tt}")
            if fc + 1 < DC:
                tg.task("sp", lambda e, fc=fc, h0=h0: e.dma_start(
                            out=dwb[h0], in_=dwc[fc + 1, :, 5504:FF]),
                        reads=["dwc"], writes=[f"dwb{h0}"], tag=f"f2wb{fc+1}.{tt}")
            tg.task("pe", f2_mm_b, reads=[f"dwb{h1}", "htile"], writes=[bname],
                    tag=f"f2mb{fc}.{tt}")
            tg.task("act", lambda e, fc=fc: e.activation(
                        dpan[:, fc * F2T:(fc + 1) * F2T], PBF[fc % 2], AF.Copy,
                        scale=dsc[:, fc:fc + 1]),
                    reads=[bname, "dsc"], writes=[f"dp{fc}"], tag=f"f2d{fc}.{tt}")
            tg.task("dve", lambda e, fc=fc: e.tensor_tensor(
                        sqf[fc % 2], dpan[:, fc * F2T:(fc + 1) * F2T],
                        dpan[:, fc * F2T:(fc + 1) * F2T], ALU.mult),
                    reads=[f"dp{fc}"], writes=[f"sqf{fc % 2}"], tag=f"f2s{fc}.{tt}")
            if fc > 0:
                tg.task("pe", lambda t, fc=fc: f2_stat(t, fc - 1),
                        reads=[f"dp{fc-1}", f"sqf{(fc-1) % 2}", "onesc"],
                        writes=["b2", "b3"], tag=f"f2t{fc-1}.{tt}")
        tg.task("pe", lambda t: f2_stat(t, DC - 1),
                reads=[f"dp{DC-1}", f"sqf{(DC-1) % 2}", "onesc"],
                writes=["b2", "b3"], tag=f"f2t{DC-1}.{tt}")
        f2_rows(f"f2r{tt}")
    for dc in range(DC):
        f2_epi_step(NT2 - 1, dc, q="gp" if dc % 2 == 0 else "sp")

    tg.emit_into(blk)
    st.close()
    return nc


_NC_CACHE = {}


def kernel(_debug=False, **inputs):
    key = "ncd" if _debug else "nc"
    if key not in _NC_CACHE:
        _NC_CACHE[key] = build_program(debug=_debug)
    nc = _NC_CACHE[key]

    x = np.ascontiguousarray(np.asarray(inputs["L0_attn_hidden_states"], np.float32))
    shared = dict(
        wq_t=prep_w_fc(np.asarray(inputs["L0_attn_Wq"], np.float32)),
        wo_t=prep_w_fc(np.asarray(inputs["L0_attn_Wo"], np.float32)),
        wk_t=prep_wkv(np.asarray(inputs["L0_attn_Wk"], np.float32)),
        wv_t=prep_wkv(np.asarray(inputs["L0_attn_Wv"], np.float32)),
        gw_t=prep_w_fc(np.asarray(inputs["L0_ffn_gate_w"], np.int32)),
        uw_t=prep_w_fc(np.asarray(inputs["L0_ffn_up_w"], np.int32)),
        dw_t=prep_wd(np.asarray(inputs["L0_ffn_down_w"], np.int32)),
        bq_c=prep_col(np.asarray(inputs["L0_attn_bq"], np.float32)),
        bo_c=prep_col(np.asarray(inputs["L0_attn_bo"], np.float32)),
        bk_r=np.ascontiguousarray(
            np.asarray(inputs["L0_attn_bk"], np.float32).reshape(1, FK)),
        bv_r=np.ascontiguousarray(
            np.asarray(inputs["L0_attn_bv"], np.float32).reshape(1, FK)),
        gs_c=prep_col(np.asarray(inputs["L0_ffn_gate_s"], np.float32).reshape(-1)),
        us_c=prep_col(np.asarray(inputs["L0_ffn_up_s"], np.float32).reshape(-1)),
        ds_c=prep_col(np.asarray(inputs["L0_ffn_down_s"], np.float32).reshape(-1)),
    )
    in_maps = []
    for core in range(NCORES):
        b, half = core // 2, core % 2
        m = dict(shared)
        m["xs1"] = prep_x_s1(x[b], half)
        m["xp2"] = prep_x_p2(x[b], half)
        in_maps.append(m)

    res = run_bass_kernel_spmd(nc, in_maps, core_ids=list(range(NCORES)))
    if _debug:
        return res
    return gather_out([r["outp"] for r in res.results])


# version 19
# speedup vs baseline: 1.0240x; 1.0240x over previous
import numpy as np

import concourse.bass as bass
import concourse.mybir as mybir
from concourse.bass_utils import run_bass_kernel_spmd

F32 = mybir.dt.float32
BF16 = mybir.dt.bfloat16
I32 = mybir.dt.int32
AF = mybir.ActivationFunctionType
ALU = mybir.AluOpType

D = 4096
DC = 32
FK = 1024
FKH = 512
HKV = 8
FF = 11008
FFC = 86
L = 4096
OWN = 2048
NBLK = 32
NOWNB = 16
P2H = 1024
F2T = 512
NT2 = 4
EPS_LN = 1e-5
EPS_ATT = 1e-10
B = 4
NCORES = 8
ARENA_N = 96256


def prep_x_s1(xb, half):
    v = xb.reshape(NBLK, 128, DC, 128).transpose(0, 3, 2, 1)
    v = np.ascontiguousarray(v).reshape(NBLK, 128, DC * 128)
    if half == 1:
        v = np.ascontiguousarray(np.concatenate([v[NOWNB:], v[:NOWNB]], axis=0))
    return v


def prep_x_p2(xb, half):
    off = half * OWN
    v = xb[off:off + OWN].reshape(2, P2H, DC, 128).transpose(0, 3, 2, 1)
    return np.ascontiguousarray(v).reshape(2, 128, DC * P2H)


def prep_w_fc(w):
    Fo, Din = w.shape
    nfc, ndc = Fo // 128, Din // 128
    v = w.reshape(nfc, 128, ndc, 128).transpose(0, 3, 2, 1)
    return np.ascontiguousarray(v).reshape(nfc, 128, ndc * 128)


def prep_wkv(w):
    v = w.reshape(FK, DC, 128).transpose(2, 1, 0)
    return np.ascontiguousarray(v).reshape(128, DC * FK)


def prep_wd(w):
    v = w.reshape(DC, 128, FFC, 128).transpose(0, 3, 2, 1)
    return np.ascontiguousarray(v).reshape(DC, 128, FFC * 128)


def prep_col(vec):
    n = vec.shape[0] // 128
    return np.ascontiguousarray(vec.reshape(n, 128).T)


def gather_out(parts):
    out = np.empty((B, L, D), np.float32)
    for core in range(NCORES):
        b, half = core // 2, core % 2
        a = parts[core].reshape(NT2, 128, DC, F2T).transpose(0, 3, 2, 1)
        out[b, half * OWN:(half + 1) * OWN, :] = a.reshape(OWN, D)
    return out


CPU_ENG = ("pe", "act", "dve")
DMA_ENG = ("sp", "gp")
ALL_ENG = CPU_ENG + DMA_ENG


class TG:
    def __init__(self, sems):
        self.sems = sems
        self.count = {k: 0 for k in ALL_ENG}
        self.writer = {}
        self.readers = {}
        self.tasks = []
        self.floor = {k: 0 for k in ALL_ENG}
        self.seen = {k: {} for k in ALL_ENG}

    def _aw(self, waits, src, key):
        skey, sval = src
        if skey == key and skey in CPU_ENG:
            return
        waits[skey] = max(waits.get(skey, 0), sval)

    def barrier(self):
        self.floor = {k: self.count[k] for k in ALL_ENG}

    def task(self, key, emit, reads=(), writes=(), tag=""):
        waits = {}
        for r in reads:
            w = self.writer.get(r)
            if w:
                self._aw(waits, w, key)
        for r in writes:
            w = self.writer.get(r)
            if w:
                self._aw(waits, w, key)
            for rd in self.readers.get(r, ()):
                self._aw(waits, rd, key)
        for fk, fv in self.floor.items():
            if fv > 0 and not (fk == key and fk in CPU_ENG):
                waits[fk] = max(waits.get(fk, 0), fv)
        seen = self.seen[key]
        waits = {k: v for k, v in waits.items() if v > seen.get(k, 0)}
        for k, v in waits.items():
            seen[k] = v
        inc = 16 if key in DMA_ENG else 1
        self.count[key] += inc
        me = (key, self.count[key])
        for r in reads:
            self.readers.setdefault(r, []).append(me)
        for r in writes:
            self.writer[r] = me
            self.readers[r] = []
        self.tasks.append((key, waits, emit, tag))

    def seed(self, new_res, old_res):
        sds = self.readers.setdefault(new_res, [])
        for o in old_res:
            w = self.writer.get(o)
            if w:
                sds.append(w)
            sds.extend(self.readers.get(o, ()))

    def emit_into(self, block):
        sems, tasks, count = self.sems, self.tasks, dict(self.count)

        def runner(key):
            def run(eng):
                for (k, waits, emit, tag) in tasks:
                    if k != key:
                        continue
                    for skey in sorted(waits):
                        eng.wait_ge(sems[skey], waits[skey])
                    last = emit(eng)
                    assert last is not None, f"task {tag} emitted nothing"
                    last.then_inc(sems[key], 16 if key in DMA_ENG else 1)
                if key == "sp":
                    eng.wait_ge(sems["sp"], count["sp"])
                    eng.wait_ge(sems["gp"], count["gp"])
            return run

        block.tensor(runner("pe"))
        block.scalar(runner("act"))
        block.vector(runner("dve"))
        block.sync(runner("sp"))
        block.gpsimd(runner("gp"))


class Arena:
    def __init__(self, tg, tensor):
        self.tg = tg
        self.t = tensor
        self.top = 0
        self.hist = []
        self.gen = 0

    def reset(self):
        self.gen += 1
        self.top = 0

    def mark(self):
        return (self.top + 15) // 16 * 16

    def seek(self, off):
        self.top = off

    def take(self, name, nels, dtype=BF16, parts=128):
        off = (self.top + 15) // 16 * 16
        end = off + nels
        assert end <= ARENA_N, f"arena overflow at {name}: {end}"
        self.top = end
        olds = [h[0] for h in self.hist
                if h[3] < self.gen and not (h[2] <= off or h[1] >= end)]
        if olds:
            self.tg.seed(name, olds)
        self.hist.append((name, off, end, self.gen))
        ap = self.t[0:parts, off:off + nels]
        if dtype == F32:
            assert nels % 2 == 0 and off % 2 == 0
            ap = ap.bitcast(F32)
        return ap


def build_program(debug=False):
    nc = bass.Bass("TRN2", target_bir_lowering=False)

    xs1 = nc.dram_tensor("xs1", [NBLK, 128, DC * 128], F32, kind="ExternalInput").ap()
    xp2 = nc.dram_tensor("xp2", [2, 128, DC * P2H], F32, kind="ExternalInput").ap()
    wq_t = nc.dram_tensor("wq_t", [DC, 128, D], F32, kind="ExternalInput").ap()
    wo_t = nc.dram_tensor("wo_t", [DC, 128, D], F32, kind="ExternalInput").ap()
    wk_t = nc.dram_tensor("wk_t", [128, DC * FK], F32, kind="ExternalInput").ap()
    wv_t = nc.dram_tensor("wv_t", [128, DC * FK], F32, kind="ExternalInput").ap()
    gw_t = nc.dram_tensor("gw_t", [FFC, 128, D], I32, kind="ExternalInput").ap()
    uw_t = nc.dram_tensor("uw_t", [FFC, 128, D], I32, kind="ExternalInput").ap()
    dw_t = nc.dram_tensor("dw_t", [DC, 128, FF], I32, kind="ExternalInput").ap()
    bq_c = nc.dram_tensor("bq_c", [128, DC], F32, kind="ExternalInput").ap()
    bo_c = nc.dram_tensor("bo_c", [128, DC], F32, kind="ExternalInput").ap()
    bk_r = nc.dram_tensor("bk_r", [1, FK], F32, kind="ExternalInput").ap()
    bv_r = nc.dram_tensor("bv_r", [1, FK], F32, kind="ExternalInput").ap()
    gs_c = nc.dram_tensor("gs_c", [128, FFC], F32, kind="ExternalInput").ap()
    us_c = nc.dram_tensor("us_c", [128, FFC], F32, kind="ExternalInput").ap()
    ds_c = nc.dram_tensor("ds_c", [128, DC], F32, kind="ExternalInput").ap()

    sk = "ExternalOutput" if debug else "Internal"
    xcc = nc.dram_tensor("xcc", [2, 128, DC * P2H], BF16, kind="Internal").ap()
    x2c = nc.dram_tensor("x2c", [DC, 128, OWN], F32, kind=sk).ap()
    xh2c = nc.dram_tensor("xh2c", [DC, 128, OWN], BF16, kind=sk).ap()
    hc = nc.dram_tensor("hc", [FFC, 128, OWN], BF16, kind=sk).ap()
    if debug:
        dbg_kvt = nc.dram_tensor("dbg_kvt", [128, FK], BF16, kind="ExternalOutput").ap()
        dbg_rows = nc.dram_tensor("dbg_rows", [1, 2 * OWN], BF16, kind="ExternalOutput").ap()
        dbg_recip = nc.dram_tensor("dbg_recip", [1, FK], BF16, kind="ExternalOutput").ap()
        dbg_attn = nc.dram_tensor("dbg_attn", [128, DC * P2H], BF16, kind="ExternalOutput").ap()
        dbg_oh = nc.dram_tensor("dbg_oh", [128, DC * P2H], BF16, kind="ExternalOutput").ap()
        dbg_qh = nc.dram_tensor("dbg_qh", [128, P2H], BF16, kind="ExternalOutput").ap()
        dbg_gw = nc.dram_tensor("dbg_gw", [128, 2 * D], BF16, kind="ExternalOutput").ap()
    dwc = nc.dram_tensor("dwc", [DC, 128, FF], BF16, kind="Internal").ap()
    outp = nc.dram_tensor("outp", [NT2, 128, DC * F2T], F32,
                          kind="ExternalOutput").ap()

    from contextlib import ExitStack
    st = ExitStack()
    T = lambda *a: st.enter_context(nc.sbuf_tensor(*a))

    arena_t = T("arena", [128, ARENA_N], BF16)
    own_invs = T("own_invs", [1, OWN], BF16)
    trow = T("trow", [1, 128], BF16)
    trow2p = [T("trow2a", [1, 128], BF16), T("trow2b", [1, 128], BF16)]
    trow3 = T("trow3", [1, 128], BF16)
    fr1 = T("fr1", [1, 128], F32)
    fr2 = T("fr2", [1, 128], F32)
    fr3 = T("fr3", [1, 128], F32)
    fr4 = T("fr4", [1, 128], F32)
    recipc = T("recipc", [128, HKV], F32)
    kvt = T("kvt", [128, FK], BF16)
    bqc = T("bqc", [128, DC], F32)
    bqnc = T("bqnc", [128, DC], F32)
    boc = T("boc", [128, DC], F32)
    gsc = T("gsc", [128, FFC], F32)
    usc = T("usc", [128, FFC], F32)
    dsc = T("dsc", [128, DC], F32)
    onesc = T("onesc", [128, 1], BF16)
    negmB = T("negmB", [128, 128], BF16)
    onesr = T("onesr", [1, 128], BF16)
    invscol = T("invscol", [128, 1], F32)
    ninvscol = T("ninvscol", [128, 1], F32)
    epsln = T("epsln", [1, 1], F32)
    epsat = T("epsat", [1, 1], F32)

    PB = [st.enter_context(nc.psum_tensor(f"pb{i}", [128, 512], F32))
          for i in range(8)]
    PBF = [p[:, :] for p in PB]
    sems = {k: st.enter_context(nc.semaphore(f"sem_{k}")) for k in ALL_ENG}
    blk = st.enter_context(nc.Block())
    tg = TG(sems)
    al = Arena(tg, arena_t)

    def mm(t, out, lhsT, rhs, start, stop):
        return t.matmul(out, lhsT, rhs, start=start, stop=stop,
                        skip_group_check=True)

    wk_i = al.take("wk_i", DC * FK)
    wv_i = al.take("wv_i", DC * FK)
    xph = [al.take(f"xph{i}", DC * 128) for i in range(2)]
    sqb = [al.take(f"sqb{i}", 2048) for i in range(2)]
    kvb_k = al.take("kvb_k", FK)
    kvb_v = al.take("kvb_v", FK)
    xcb = [al.take(f"xcb{i}", 4096) for i in range(2)]
    kr = al.take("kr", 1024, F32)
    krn = al.take("krn", 1024, F32)
    ke = al.take("ke", 1024, F32)
    bkr = al.take("bkr", FK, parts=1)
    bvr = al.take("bvr", FK, parts=1)
    ksum = al.take("ksum", 2 * FK, F32, parts=1)
    reciprow = al.take("reciprow", FK, parts=1)

    tg.task("dve", lambda e: e.memset(onesc[:, :], 1.0), writes=["onesc"], tag="oc")
    tg.task("dve", lambda e: e.memset(onesr[:, :], 1.0), writes=["onesr"], tag="or")
    tg.task("dve", lambda e: e.memset(ksum[:, :], 0.0), writes=["ksum"], tag="ks0")
    tg.task("dve", lambda e: e.memset(epsln[:, :], EPS_LN), writes=["epsc"], tag="ep1")
    tg.task("dve", lambda e: e.memset(epsat[:, :], EPS_ATT), writes=["epsc"], tag="ep2")
    tg.task("gp", lambda e: e.dma_start(out=wk_i, in_=wk_t[:, :]), writes=["wk_i"],
            tag="lwk")
    tg.task("gp", lambda e: e.dma_start(out=wv_i, in_=wv_t[:, :]), writes=["wv_i"],
            tag="lwv")
    tg.task("gp", lambda e: e.dma_start(out=bkr, in_=bk_r[:, :]), writes=["bkr"],
            tag="lbk")
    tg.task("gp", lambda e: e.dma_start(out=bvr, in_=bv_r[:, :]), writes=["bvr"],
            tag="lbv")
    tg.task("sp", lambda e: e.dma_start(out=bqc[:, :], in_=bq_c[:, :]),
            writes=["bqc"], tag="lbq")
    tg.task("sp", lambda e: e.dma_start(out=boc[:, :], in_=bo_c[:, :]),
            writes=["boc"], tag="lbo")
    tg.task("sp", lambda e: e.dma_start(out=gsc[:, :], in_=gs_c[:, :]),
            writes=["gsc"], tag="lgs")
    tg.task("sp", lambda e: e.dma_start(out=usc[:, :], in_=us_c[:, :]),
            writes=["usc"], tag="lus")
    tg.task("sp", lambda e: e.dma_start(out=dsc[:, :], in_=ds_c[:, :]),
            writes=["dsc"], tag="lds")
    tg.task("act", lambda e: e.activation(bqnc[:, :], bqc[:, :], AF.Copy,
                                          scale=-1.0),
            reads=["bqc"], writes=["bqnc"], tag="nbq")


    def s1_load(b2):
        return lambda e: e.dma_start(out=xph[b2 % 2], in_=xs1[b2, :, :])

    def s1_sq(b2, hlf):
        par = b2 % 2
        sl = slice(hlf * 2048, (hlf + 1) * 2048)
        return lambda e: e.tensor_tensor(sqb[hlf], xph[par][:, sl], xph[par][:, sl],
                                         ALU.mult)

    def s1_stats(b2, hlf):
        par = b2 % 2
        def f(t):
            last = None
            for j in range(16):
                dc = hlf * 16 + j
                mm(t, PB[0][0:1, 0:128], onesc[:, :],
                   xph[par][:, dc * 128:(dc + 1) * 128], hlf == 0 and j == 0, False)
                last = mm(t, PB[0][0:1, 128:256], onesc[:, :],
                          sqb[hlf][:, j * 128:(j + 1) * 128], False,
                          hlf == 1 and j == 15)
            return last
        return f

    def s1_rows_a(e):
        e.activation(trow[:, :], PB[0][0:1, 0:128], AF.Copy, scale=-1.0 / D)
        e.activation(fr1[:, :], PB[0][0:1, 0:128], AF.Copy, scale=1.0 / D)
        return e.activation(fr2[:, :], PB[0][0:1, 128:256], AF.Copy, scale=1.0 / D)

    def s1_rows_b(e):
        e.tensor_tensor(fr3[:, :], fr1[:, :], fr1[:, :], ALU.mult)
        return e.tensor_tensor(fr4[:, :], fr2[:, :], fr3[:, :], ALU.subtract)

    def s1_rows_c(par):
        def f(e):
            e.activation(fr3[:, :], fr4[:, :], AF.Ln, bias=epsln[:, :])
            e.activation(trow2p[par][:, :], fr3[:, :], AF.Exp, scale=0.5)
            return e.activation(trow3[:, :], fr3[:, :], AF.Exp, scale=-0.5)
        return f

    def s1_rows_own(b2):
        o = b2 * 128
        def f(e):
            return e.activation(own_invs[0:1, o:o + 128], trow3[:, :], AF.Copy)
        return f

    def s1_invcol(t):
        return mm(t, PB[0][:, 0:1], trow3[:, :], onesr[0:1, 0:1], True, True)

    def s1_invev(e):
        e.activation(invscol[:, :], PB[0][:, 0:1], AF.Copy)
        return e.activation(ninvscol[:, :], PB[0][:, 0:1], AF.Copy, scale=-1.0)

    def s1_kv_k(b2):
        par = b2 % 2
        def f(t):
            for dc in range(DC):
                lh = xcb[par][:, dc * 128:(dc + 1) * 128]
                s0 = dc == 0
                mm(t, PBF[2], lh, wk_i[:, dc * FK: dc * FK + FKH], s0, False)
                mm(t, PBF[3], lh, wk_i[:, dc * FK + FKH: dc * FK + FK], s0, False)
            mm(t, PBF[2], trow2p[par][:, :], bkr[0:1, 0:FKH], False, True)
            return mm(t, PBF[3], trow2p[par][:, :], bkr[0:1, FKH:FK], False, True)
        return f

    def s1_kv_v(b2):
        par = b2 % 2
        def f(t):
            for dc in range(DC):
                lh = xcb[par][:, dc * 128:(dc + 1) * 128]
                s0 = dc == 0
                mm(t, PBF[4], lh, wv_i[:, dc * FK: dc * FK + FKH], s0, False)
                mm(t, PBF[5], lh, wv_i[:, dc * FK + FKH: dc * FK + FK], s0, False)
            mm(t, PBF[4], trow2p[par][:, :], bvr[0:1, 0:FKH], False, True)
            return mm(t, PBF[5], trow2p[par][:, :], bvr[0:1, FKH:FK], False, True)
        return f

    def s1_xbc(t):
        return mm(t, PBF[1][:, 0:128], onesr[:, :], trow[:, :], True, True)

    def s1_xbev(e):
        return e.activation(negmB[:, :], PBF[1][:, 0:128], AF.Copy)

    def s1_xc(b2):
        par = b2 % 2
        nb = negmB[:, :].rearrange("p u -> p () u").to_broadcast((128, 32, 128))
        def f(e):
            return e.tensor_tensor(
                xcb[par][:, :].rearrange("p (c u) -> p c u", u=128),
                xph[par][:, :].rearrange("p (c u) -> p c u", u=128), nb, ALU.add)
        return f

    def s1_xcw(b2):
        par = b2 % 2
        hf, t0 = b2 // 8, (b2 % 8) * 128
        def f(e):
            ap = xcc[hf, :, :].rearrange("p (c t) -> p c t", c=DC)
            return e.dma_start(
                out=ap[:, :, t0:t0 + 128],
                in_=xcb[par][:, :].rearrange("p (c u) -> p c u", u=128))
        return f

    def s1_kev(e):
        e.activation(kr[:, :], PBF[2], AF.Relu, scale=invscol[:, :])
        e.activation(krn[:, :], PBF[2], AF.Relu, scale=ninvscol[:, :])
        return e.activation(ke[:, :], krn[:, :], AF.Exp, scale=-1.0)

    def s1_vev(e):
        e.activation(kvb_v[:, 0:FKH], PBF[4], AF.Copy, scale=invscol[:, :])
        return e.activation(kvb_v[:, FKH:FK], PBF[5], AF.Copy, scale=invscol[:, :])

    def s1_kev2(e):
        e.activation(kr[:, :], PBF[3], AF.Relu, scale=invscol[:, :])
        e.activation(krn[:, :], PBF[3], AF.Relu, scale=ninvscol[:, :])
        return e.activation(ke[:, :], krn[:, :], AF.Exp, scale=-1.0)

    def s1_kadd(hlf):
        return lambda e: e.tensor_tensor(kvb_k[:, hlf * FKH:(hlf + 1) * FKH],
                                         kr[:, :], ke[:, :], ALU.add)

    def s1_kvt(b2):
        def f(t):
            for h in range(HKV):
                bank = PBF[6] if h < 4 else PBF[7]
                col = (h % 4) * 128
                mm(t, bank[:, col:col + 128], kvb_k[:, h * 128:(h + 1) * 128],
                   kvb_v[:, h * 128:(h + 1) * 128],
                   b2 == 0 and h % 4 == 0, b2 == NBLK - 1)
            mm(t, PB[2][0:1, 0:512], onesc[:, :], kvb_k[:, 0:FKH], True, True)
            return mm(t, PB[3][0:1, 0:512], onesc[:, :], kvb_k[:, FKH:FK], True, True)
        return f

    def s1_ksacc(e):
        e.tensor_tensor(ksum[0:1, 0:FKH], ksum[0:1, 0:FKH], PB[2][0:1, 0:512],
                        ALU.add)
        return e.tensor_tensor(ksum[0:1, FKH:FK], ksum[0:1, FKH:FK],
                               PB[3][0:1, 0:512], ALU.add)

    tg.task("gp", s1_load(0), writes=["xph0"], tag="s1l0")
    tg.task("gp", s1_load(1), writes=["xph1"], tag="s1l1")
    for hlf in range(2):
        tg.task("dve", s1_sq(0, hlf), reads=["xph0"], writes=[f"sqb{hlf}"],
                tag=f"s1q0.{hlf}")
        tg.task("pe", s1_stats(0, hlf), reads=["xph0", f"sqb{hlf}", "onesc"],
                writes=["b0"], tag=f"s1t0.{hlf}")
    tg.task("act", s1_rows_a, reads=["b0"], writes=["frows", "trows"], tag="s1a0")
    tg.task("dve", s1_rows_b, reads=["frows"], writes=["frows"], tag="s1b0")
    tg.task("act", s1_rows_c(0), reads=["frows", "epsc"],
            writes=["trows", "trow2a"], tag="s1c0")
    tg.task("act", s1_rows_own(0), reads=["trows"], writes=["ownrows"], tag="s1o0")
    tg.task("pe", s1_xbc, reads=["trows", "onesr"], writes=["b1"], tag="s1xb0")
    tg.task("act", s1_xbev, reads=["b1"], writes=["negmB"], tag="s1xe0")
    tg.task("dve", s1_xc(0), reads=["xph0", "negmB"], writes=["xcb0"],
            tag="s1xc0")
    tg.task("sp", s1_xcw(0), reads=["xcb0"], writes=["xcc"], tag="s1xw0")
    for b2 in range(NBLK):
        nxt = (b2 + 1) % 2
        if b2 + 2 < NBLK:
            tg.task("gp", s1_load(b2 + 2), writes=[f"xph{b2 % 2}"],
                    tag=f"s1l{b2+2}")
        if b2 == 0:
            tg.task("pe", s1_invcol, reads=["trows", "onesr"], writes=["b0"],
                    tag="s1i0")
            tg.task("act", s1_invev, reads=["b0"], writes=["invc"], tag="s1ie0")
        if b2 + 1 < NBLK:
            for hlf in range(2):
                tg.task("dve", s1_sq(b2 + 1, hlf), reads=[f"xph{nxt}"],
                        writes=[f"sqb{hlf}"], tag=f"s1q{b2+1}.{hlf}")
                tg.task("pe", s1_stats(b2 + 1, hlf),
                        reads=[f"xph{nxt}", f"sqb{hlf}", "onesc"], writes=["b0"],
                        tag=f"s1t{b2+1}.{hlf}")
            tg.task("act", s1_rows_a, reads=["b0"], writes=["frows", "trows"],
                    tag=f"s1a{b2+1}")
            tg.task("dve", s1_rows_b, reads=["frows"], writes=["frows"],
                    tag=f"s1b{b2+1}")
            tg.task("act", s1_rows_c((b2 + 1) % 2), reads=["frows", "epsc"],
                    writes=["trows", f"trow2{'ab'[(b2+1) % 2]}"],
                    tag=f"s1c{b2+1}")
            if b2 + 1 < NOWNB:
                tg.task("act", s1_rows_own(b2 + 1), reads=["trows"],
                        writes=["ownrows"], tag=f"s1o{b2+1}")
        tg.task("pe", s1_kv_k(b2), reads=[f"xcb{b2 % 2}", "wk_i",
                                          f"trow2{'ab'[b2 % 2]}", "bkr"],
                writes=["b2", "b3"], tag=f"s1kk{b2}")
        tg.task("act", s1_kev, reads=["b2", "b3", "invc"], writes=["phis"],
                tag=f"s1ke{b2}")
        tg.task("dve", s1_kadd(0), reads=["phis"], writes=["kvb_k"], tag=f"s1ka{b2}")
        tg.task("act", s1_kev2, reads=["b3", "invc"], writes=["phis"],
                tag=f"s1k2{b2}")
        tg.task("dve", s1_kadd(1), reads=["phis"], writes=["kvb_k"], tag=f"s1kb{b2}")
        tg.task("pe", s1_kv_v(b2), reads=[f"xcb{b2 % 2}", "wv_i",
                                          f"trow2{'ab'[b2 % 2]}", "bvr"],
                writes=["b4", "b5"], tag=f"s1kv{b2}")
        tg.task("act", s1_vev, reads=["b4", "b5", "invc"], writes=["kvb_v"],
                tag=f"s1ve{b2}")
        if b2 + 1 < NBLK:
            tg.task("pe", s1_xbc, reads=["trows", "onesr"], writes=["b1"],
                    tag=f"s1xb{b2+1}")
            tg.task("act", s1_xbev, reads=["b1"], writes=["negmB"],
                    tag=f"s1xe{b2+1}")
            tg.task("pe", s1_invcol, reads=["trows", "onesr"], writes=["b0"],
                    tag=f"s1i{b2+1}")
            tg.task("act", s1_invev, reads=["b0"], writes=["invc"],
                    tag=f"s1ie{b2+1}")
            tg.task("dve", s1_xc(b2 + 1), reads=[f"xph{nxt}", "negmB"],
                    writes=[f"xcb{nxt}"], tag=f"s1xc{b2+1}")
            if b2 + 1 < NOWNB:
                tg.task("sp", s1_xcw(b2 + 1), reads=[f"xcb{nxt}"],
                        writes=["xcc"], tag=f"s1xw{b2+1}")
        tg.task("pe", s1_kvt(b2), reads=["kvb_k", "kvb_v", "onesc"],
                writes=["b6", "b7", "b2", "b3"], tag=f"s1v{b2}")
        tg.task("dve", s1_ksacc, reads=["b2", "b3"], writes=["ksum"], tag=f"s1s{b2}")

    tg.task("act", lambda e: (e.activation(kvt[:, 0:512], PBF[6], AF.Copy),
                              e.activation(kvt[:, 512:FK], PBF[7], AF.Copy))[-1],
            reads=["b6", "b7"], writes=["kvt"], tag="kvte")
    tg.task("act", lambda e: (e.activation(ke[0:1, 0:512], ksum[0:1, 0:FKH],
                                           AF.Copy, bias=EPS_ATT),
                              e.activation(krn[0:1, 0:512], ksum[0:1, FKH:FK],
                                           AF.Copy, bias=EPS_ATT))[-1],
            reads=["ksum"], writes=["phis"], tag="kse")
    tg.task("dve", lambda e: (e.reciprocal(kr[0:1, 0:512], ke[0:1, 0:512]),
                              e.reciprocal(ke[0:1, 0:512], krn[0:1, 0:512]))[-1],
            reads=["phis"], writes=["phis"], tag="ksr")
    tg.task("act", lambda e: (e.activation(reciprow[0:1, 0:FKH], kr[0:1, 0:512], AF.Copy),
                              e.activation(reciprow[0:1, FKH:FK], ke[0:1, 0:512],
                                           AF.Copy))[-1],
            reads=["phis"], writes=["reciprow"], tag="ksc2")

    def recip_cols(t):
        last = None
        for h in range(HKV):
            last = mm(t, PB[0][:, h:h + 1], reciprow[0:1, h * 128:(h + 1) * 128],
                      onesr[0:1, 0:1], h == 0, h == HKV - 1)
        return last
    tg.task("pe", recip_cols, reads=["reciprow", "onesr"], writes=["b0"], tag="rc")
    tg.task("act", lambda e: e.activation(recipc[:, :], PB[0][:, 0:HKV], AF.Copy),
            reads=["b0"], writes=["recipc"], tag="rce")

    if debug:
        tg.task("sp", lambda e: e.dma_start(out=dbg_kvt[:, :], in_=kvt[:, :]),
                reads=["kvt"], writes=["dbg"], tag="dkvt")
        tg.task("sp", lambda e: e.dma_start(out=dbg_rows[0:1, OWN:2 * OWN],
                                            in_=own_invs[:, :]),
                reads=["ownrows"], writes=["dbg"], tag="drow2")
        tg.task("sp", lambda e: e.dma_start(out=dbg_recip[:, :], in_=reciprow),
                reads=["reciprow"], writes=["dbg"], tag="drec")
    tg.barrier()

    al.reset()
    xpo = al.take("xpo", DC * P2H)
    ATTN = al.take("ATTN", DC * P2H)
    wqb = [al.take(f"wqb{i}", D) for i in range(2)]
    epi_base = al.mark()
    qh = [al.take(f"qh{i}", P2H) for i in range(2)]
    u_s = al.take("u_s", 2 * P2H, F32)
    r_s = al.take("r_s", 2 * P2H, F32)
    rn_s = al.take("rn_s", 2 * P2H, F32)
    invsb = al.take("invsb", 2 * P2H, F32)
    stat_end = al.mark()
    browR = al.take("browR", P2H, parts=1)
    browC = al.take("browC", P2H, parts=1)
    frA_off = al.mark()
    frA = al.take("frA", 2 * P2H, F32, parts=1)
    frB = al.take("frB", 2 * P2H, F32, parts=1)
    al.gen += 1
    al.seek(frA_off)
    pls = [al.take(f"pls{i}", 2 * P2H, F32) for i in range(2)]
    al.gen -= 1
    tail_base = al.mark()
    al.gen += 1
    al.seek(epi_base)
    xres = [al.take(f"xres{i}", 2 * P2H, F32) for i in range(2)]
    x2b = [al.take(f"x2b{i}", 2 * P2H, F32) for i in range(2)]
    sqc = [al.take(f"sqc{i}", P2H) for i in range(2)]
    assert al.mark() <= stat_end
    al.seek(tail_base)
    xh2w = [al.take(f"xh2w{i}", P2H) for i in range(2)]
    rb_b = al.take("rb_b", 2 * P2H, F32)
    cb_b = al.take("cb_b", 2 * P2H, F32)
    assert al.top <= ARENA_N

    def p2_rows(tag):
        def a(e):
            e.activation(frA[0:1, 0:512], PB[6][0:1, 0:512], AF.Copy, scale=1.0 / D)
            e.activation(frA[0:1, 512:P2H], PB[7][0:1, 0:512], AF.Copy,
                         scale=1.0 / D)
            e.activation(frB[0:1, 0:512], PB[4][0:1, 0:512], AF.Copy, scale=1.0 / D)
            return e.activation(frB[0:1, 512:P2H], PB[5][0:1, 0:512], AF.Copy,
                                scale=1.0 / D)

        def b(e):
            e.tensor_tensor(PB[4][0:1, 0:512], frA[0:1, 0:512], frA[0:1, 0:512],
                            ALU.mult)
            e.tensor_tensor(PB[5][0:1, 0:512], frA[0:1, 512:P2H],
                            frA[0:1, 512:P2H], ALU.mult)
            e.tensor_tensor(PB[6][0:1, 0:512], frB[0:1, 0:512], PB[4][0:1, 0:512],
                            ALU.subtract)
            return e.tensor_tensor(PB[7][0:1, 0:512], frB[0:1, 512:P2H],
                                   PB[5][0:1, 0:512], ALU.subtract)

        def c(e):
            e.activation(PB[2][0:1, 0:512], PB[6][0:1, 0:512], AF.Ln, bias=epsln[:, :])
            e.activation(PB[3][0:1, 0:512], PB[7][0:1, 0:512], AF.Ln, bias=epsln[:, :])
            e.activation(PB[4][0:1, 0:512], PB[2][0:1, 0:512], AF.Exp, scale=-0.5)
            e.activation(PB[5][0:1, 0:512], PB[3][0:1, 0:512], AF.Exp, scale=-0.5)
            e.activation(browR[0:1, 0:512], PB[4][0:1, 0:512], AF.Copy)
            return e.activation(browR[0:1, 512:P2H], PB[5][0:1, 0:512], AF.Copy)

        def d(e):
            e.tensor_tensor(PB[6][0:1, 0:512], frA[0:1, 0:512], PB[4][0:1, 0:512],
                            ALU.mult)
            return e.tensor_tensor(PB[7][0:1, 0:512], frA[0:1, 512:P2H],
                                   PB[5][0:1, 0:512], ALU.mult)

        def d2(e):
            e.activation(browC[0:1, 0:512], PB[6][0:1, 0:512], AF.Copy, scale=-1.0)
            return e.activation(browC[0:1, 512:P2H], PB[7][0:1, 0:512], AF.Copy,
                                scale=-1.0)

        tg.task("act", a, reads=["b4", "b5", "b6", "b7"], writes=["frAB"],
                tag=tag + "a")
        tg.task("dve", b, reads=["frAB"], writes=["b4", "b5", "b6", "b7"],
                tag=tag + "b")
        tg.task("act", c, reads=["b6", "b7", "epsc"],
                writes=["b2", "b3", "b4", "b5", "browR"], tag=tag + "c")
        tg.task("dve", d, reads=["frAB", "b4", "b5"], writes=["b6", "b7"],
                tag=tag + "d")
        tg.task("act", d2, reads=["b6", "b7"], writes=["browC"], tag=tag + "e")

    def p2_bcast(tag):
        def f(t):
            mm(t, PBF[6], onesr[:, :], browR[0:1, 0:512], True, True)
            mm(t, PBF[7], onesr[:, :], browR[0:1, 512:P2H], True, True)
            mm(t, PBF[4], onesr[:, :], browC[0:1, 0:512], True, True)
            return mm(t, PBF[5], onesr[:, :], browC[0:1, 512:P2H], True, True)

        def ev(e):
            e.activation(rb_b[:, 0:512], PBF[6], AF.Copy)
            e.activation(rb_b[:, 512:P2H], PBF[7], AF.Copy)
            e.activation(cb_b[:, 0:512], PBF[4], AF.Copy)
            return e.activation(cb_b[:, 512:P2H], PBF[5], AF.Copy)

        tg.task("pe", f, reads=["browR", "browC", "onesr"],
                writes=["b4", "b5", "b6", "b7"], tag=tag + "m")
        tg.task("act", ev, reads=["b4", "b5", "b6", "b7"], writes=["rbcb"],
                tag=tag + "v")

    qpair = [((PBF[0], "b0"), (PBF[1], "b1")), ((PBF[2], "b2"), (PBF[3], "b3"))]
    _f1w_pre = [arena_t[0:128, off:off + D]
                for off in (65536, 69632, 73728, 77824)]

    def f1_prefetch():
        tg.seed("gwb0", ["wqb0"])
        tg.seed("gwb1", ["wqb1"])
        tg.task("gp", lambda e: e.dma_start(out=_f1w_pre[0], in_=gw_t[0, :, :]),
                reads=["wqb0"], writes=["gwb0"], tag="f1pg0")
        tg.task("gp", lambda e: e.dma_start(out=_f1w_pre[1], in_=gw_t[1, :, :]),
                reads=["wqb1"], writes=["gwb1"], tag="f1pg1")

    def p2_half(hf, eh_hook=None, defer_pass2=False):
        hoff = hf * P2H
        ohat = xpo
        xh2pan = ATTN
        if hf == 1:
            tg.seed("qh0", ["xres0"])
            tg.seed("qh1", ["xres0"])
            tg.seed("u_s", ["xres1"])
            tg.seed("r_s", ["x2b0"])
            tg.seed("rn_s", ["x2b1"])
            tg.seed("invsb", ["sqc0", "sqc1"])

        if hf == 0:
            for dc in range(DC):
                tg.seed(f"xpo{dc}", ["xpo"])
                tg.task("sp", lambda e, dc=dc: e.dma_start(
                            out=xpo[:, dc * P2H:(dc + 1) * P2H],
                            in_=xcc[0, :, dc * P2H:(dc + 1) * P2H]),
                        reads=["xcc"], writes=[f"xpo{dc}"], tag=f"p2x0.{dc}")

        def invsb_mm(t):
            mm(t, PBF[6], onesr[:, :], own_invs[0:1, hoff:hoff + 512], True, True)
            return mm(t, PBF[7], onesr[:, :], own_invs[0:1, hoff + 512:hoff + P2H],
                      True, True)
        tg.task("pe", invsb_mm, reads=["ownrows", "onesr"], writes=["b6", "b7"],
                tag=f"p2i{hf}")
        tg.task("act", lambda e: (e.activation(invsb[:, 0:512], PBF[6], AF.Copy),
                                  e.activation(invsb[:, 512:P2H], PBF[7],
                                               AF.Copy))[-1],
                reads=["b6", "b7"], writes=["invsb"], tag=f"p2ie{hf}")

        def wq_load(fc):
            return lambda e: e.dma_start(out=wqb[fc % 2], in_=wq_t[fc, :, :])

        def q_mm_h(fc, hlf):
            (pa, _), (pb_, _) = qpair[fc % 2]
            def f(t):
                last = None
                for dc in range(hlf * 16, hlf * 16 + 16):
                    lh = wqb[fc % 2][:, dc * 128:(dc + 1) * 128]
                    mm(t, pa, lh, xpo[:, dc * P2H: dc * P2H + 512], dc == 0,
                       dc == DC - 1)
                    last = mm(t, pb_, lh, xpo[:, dc * P2H + 512:(dc + 1) * P2H],
                              dc == 0, dc == DC - 1)
                return last
            return f

        def q_u(fc):
            (pa, _), (pb_, _) = qpair[fc % 2]
            def f(e):
                e.tensor_tensor(u_s[:, 0:512], pa, invsb[:, 0:512], ALU.mult)
                return e.tensor_tensor(u_s[:, 512:P2H], pb_, invsb[:, 512:P2H],
                                       ALU.mult)
            return f

        def q_phi_a(fc):
            def f(e):
                e.activation(r_s, u_s, AF.Relu, bias=bqc[:, fc:fc + 1])
                e.activation(rn_s, u_s, AF.Relu, scale=-1.0, bias=bqnc[:, fc:fc + 1])
                return e.activation(u_s, rn_s, AF.Exp, scale=-1.0)
            return f

        def q_phi_b(fc):
            return lambda e: e.tensor_tensor(qh[fc % 2], r_s, u_s, ALU.add)

        def at_mm(fc):
            kvh = fc // 4
            def f(t):
                lh = kvt[:, kvh * 128:(kvh + 1) * 128]
                mm(t, PBF[4], lh, qh[fc % 2][:, 0:512], True, True)
                return mm(t, PBF[5], lh, qh[fc % 2][:, 512:P2H], True, True)
            return f

        def at_ev(fc):
            kvh = fc // 4
            def f(e):
                e.activation(ATTN[:, fc * P2H: fc * P2H + 512], PBF[4], AF.Copy,
                             scale=recipc[:, kvh:kvh + 1])
                return e.activation(ATTN[:, fc * P2H + 512:(fc + 1) * P2H], PBF[5],
                                    AF.Copy, scale=recipc[:, kvh:kvh + 1])
            return f

        def q_evac(fc):
            (_, na), (_, nb) = qpair[fc % 2]
            tg.task("dve", q_u(fc), reads=[na, nb, "invsb"], writes=["u_s"],
                    tag=f"pu{fc}.{hf}")
            tg.task("act", q_phi_a(fc), reads=["u_s", "bqc", "bqnc"],
                    writes=["r_s", "rn_s", "u_s"], tag=f"pp{fc}.{hf}")
            tg.task("dve", q_phi_b(fc), reads=["r_s", "u_s"],
                    writes=[f"qh{fc % 2}"], tag=f"pf{fc}.{hf}")

        def at_emit(fc):
            if hf == 1:
                tg.seed(f"atc{fc}", [f"xh2p{fc}"])
            tg.task("pe", at_mm(fc), reads=["kvt", f"qh{fc % 2}"],
                    writes=["b4", "b5"], tag=f"pa{fc}.{hf}")
            tg.task("act", at_ev(fc), reads=["b4", "b5", "recipc"],
                    writes=["ATTN", f"atc{fc}"], tag=f"pe{fc}.{hf}")

        if hf == 0:
            tg.task("gp", wq_load(0), writes=["wqb0"], tag=f"pw0.{hf}")
            tg.task("gp", wq_load(1), writes=["wqb1"], tag=f"pw1.{hf}")
        for fc in range(DC):
            (_, na), (_, nb) = qpair[fc % 2]
            tg.task("pe", q_mm_h(fc, 0),
                    reads=[f"wqb{fc % 2}"] + [f"xpo{d}" for d in range(16)],
                    writes=[na, nb], tag=f"pqa{fc}.{hf}")
            tg.task("pe", q_mm_h(fc, 1),
                    reads=[f"wqb{fc % 2}"] + [f"xpo{d}" for d in range(16, DC)],
                    writes=[na, nb], tag=f"pqb{fc}.{hf}")
            if fc + 2 < DC:
                tg.task("gp", wq_load(fc + 2), writes=[f"wqb{fc % 2}"],
                        tag=f"pw{fc+2}.{hf}")
            if eh_hook is not None:
                eh_hook(fc)
            if fc > 0:
                at_emit(fc - 1)
            q_evac(fc)
        at_emit(DC - 1)

        for dc in range(DC):
            tg.seed(f"oh{dc}", [f"xpo{dc}"])

        def wo_load(fc):
            return lambda e: e.dma_start(out=wqb[fc % 2], in_=wo_t[fc, :, :])

        def o_mm(fc):
            (pa, _), (pb_, _) = qpair[fc % 2]
            def f(t):
                last = None
                for cc in range(DC):
                    lh = wqb[fc % 2][:, cc * 128:(cc + 1) * 128]
                    mm(t, pa, lh, ATTN[:, cc * P2H: cc * P2H + 512], cc == 0, False)
                    last = mm(t, pb_, lh, ATTN[:, cc * P2H + 512:(cc + 1) * P2H],
                              cc == 0, cc == DC - 1)
                return last
            return f

        def o_ev(fc):
            (pa, _), (pb_, _) = qpair[fc % 2]
            def f(e):
                e.activation(ohat[:, fc * P2H: fc * P2H + 512], pa, AF.Identity,
                             bias=boc[:, fc:fc + 1])
                return e.activation(ohat[:, fc * P2H + 512:(fc + 1) * P2H], pb_,
                                    AF.Identity, bias=boc[:, fc:fc + 1])
            return f

        def p_sq(idx, pan):
            def f(e):
                sl = slice(idx * P2H, (idx + 1) * P2H)
                return e.tensor_tensor(sqc[idx % 2], pan[:, sl], pan[:, sl],
                                       ALU.mult)
            return f

        def p_stat(idx, pan):
            def f(t):
                sl = slice(idx * P2H, idx * P2H + 512)
                s2 = slice(idx * P2H + 512, (idx + 1) * P2H)
                mm(t, PB[6][0:1, 0:512], onesc[:, :], pan[:, sl], idx == 0, False)
                mm(t, PB[7][0:1, 0:512], onesc[:, :], pan[:, s2], idx == 0, False)
                mm(t, PB[4][0:1, 0:512], onesc[:, :], sqc[idx % 2][:, 0:512],
                   idx == 0, False)
                return mm(t, PB[5][0:1, 0:512], onesc[:, :],
                          sqc[idx % 2][:, 512:P2H], idx == 0, idx == DC - 1)
            return f

        tg.task("gp", wo_load(0), writes=["wqb0"], tag=f"po0.{hf}")
        for fc in range(DC):
            (_, na), (_, nb) = qpair[fc % 2]
            if fc + 1 < DC:
                tg.task("gp", wo_load(fc + 1), writes=[f"wqb{(fc+1) % 2}"],
                        tag=f"po{fc+1}.{hf}")
            tg.task("pe", o_mm(fc), reads=[f"wqb{fc % 2}", "ATTN"],
                    writes=[na, nb], tag=f"pm{fc}.{hf}")
            tg.task("act", o_ev(fc), reads=[na, nb, "boc"], writes=[f"oh{fc}"],
                    tag=f"pv{fc}.{hf}")
            tg.task("dve", p_sq(fc, ohat), reads=[f"oh{fc}"],
                    writes=[f"sqc{fc % 2}"], tag=f"ps{fc}.{hf}")
            if fc > 0:
                tg.task("pe", p_stat(fc - 1, ohat),
                        reads=[f"oh{fc-1}", f"sqc{(fc-1) % 2}", "onesc"],
                        writes=["b4", "b5", "b6", "b7"], tag=f"pz{fc-1}.{hf}")
        tg.task("pe", p_stat(DC - 1, ohat),
                reads=[f"oh{DC-1}", f"sqc{(DC-1) % 2}", "onesc"],
                writes=["b4", "b5", "b6", "b7"], tag=f"pz{DC-1}.{hf}")
        if hf == 0:
            tg.task("gp", wq_load(0), writes=["wqb0"], tag="pw0.1p")
            tg.task("gp", wq_load(1), writes=["wqb1"], tag="pw1.1p")

        if hf == 1:
            f1_prefetch()
        p2_rows(f"lo{hf}")
        p2_bcast(f"lb{hf}")
        if debug and hf == 0:
            tg.task("sp", lambda e: e.dma_start(out=dbg_attn[:, :], in_=ATTN),
                    reads=["ATTN"], writes=["dbg"], tag="dattn")
            tg.task("sp", lambda e: e.dma_start(out=dbg_oh[:, :], in_=ohat),
                    reads=[f"oh{d}" for d in range(DC)], writes=["dbg"], tag="doh")
            tg.task("sp", lambda e: e.dma_start(out=dbg_qh[:, :], in_=qh[(DC - 1) % 2]),
                    reads=[f"qh{(DC-1) % 2}"], writes=["dbg"], tag="dqh")
        tg.barrier()

        for d in range(DC):
            tg.seed(f"xh2p{d}", ["ATTN"])

        def ep1_xload(dc):
            return lambda e: e.dma_start(out=xres[dc % 2],
                                         in_=xp2[hf, :, dc * P2H:(dc + 1) * P2H])

        def ep1_y1(dc):
            def f(e):
                sl = slice(dc * P2H, (dc + 1) * P2H)
                e.tensor_tensor(PBF[0], ohat[:, sl][:, 0:512], rb_b[:, 0:512],
                                ALU.mult)
                e.tensor_tensor(PBF[1], ohat[:, sl][:, 512:P2H], rb_b[:, 512:P2H],
                                ALU.mult)
                e.tensor_tensor(PBF[2], PBF[0], cb_b[:, 0:512], ALU.add)
                e.tensor_tensor(PBF[3], PBF[1], cb_b[:, 512:P2H], ALU.add)
                e.tensor_tensor(x2b[dc % 2][:, 0:512], PBF[2],
                                xres[dc % 2][:, 0:512], ALU.add)
                return e.tensor_tensor(x2b[dc % 2][:, 512:P2H], PBF[3],
                                       xres[dc % 2][:, 512:P2H], ALU.add)
            return f

        def ep1_cast(dc):
            return lambda e: e.activation(xh2pan[:, dc * P2H:(dc + 1) * P2H],
                                          x2b[dc % 2], AF.Copy)

        def ep1_x2w(dc):
            return lambda e: e.dma_start(out=x2c[dc, :, hf * P2H:(hf + 1) * P2H],
                                         in_=x2b[dc % 2])

        def ep1_y1p(dc):
            def f(e):
                sl = slice(dc * P2H, (dc + 1) * P2H)
                e.tensor_tensor(pls[0], ohat[:, sl], rb_b, ALU.mult)
                e.tensor_tensor(pls[1], pls[0], cb_b, ALU.add)
                return e.tensor_tensor(x2b[dc % 2], pls[1], xres[dc % 2], ALU.add)
            return f

        tg.seed("pls", ["frAB"])
        for dc in range(DC):
            tg.task("gp", ep1_xload(dc), writes=[f"xres{dc % 2}"],
                    tag=f"ex{dc}.{hf}")
            tg.task("dve", ep1_y1p(dc), reads=[f"oh{dc}", "rbcb",
                                               f"xres{dc % 2}"],
                    writes=[f"x2b{dc % 2}", "pls"], tag=f"ey{dc}.{hf}")
            if hf == 0:
                tg.task("sp", lambda e, dc=dc: e.dma_start(
                            out=xpo[:, dc * P2H:(dc + 1) * P2H],
                            in_=xcc[1, :, dc * P2H:(dc + 1) * P2H]),
                        reads=["xcc"], writes=[f"xpo{dc}", f"oh{dc}"],
                        tag=f"p2x1.{dc}")
            tg.task("act", ep1_cast(dc), reads=[f"x2b{dc % 2}"],
                    writes=[f"xh2p{dc}"], tag=f"ec{dc}.{hf}")
            if dc > 0:
                tg.task("dve", p_sq(dc - 1, xh2pan), reads=[f"xh2p{dc-1}"],
                        writes=[f"sqc{(dc-1) % 2}"], tag=f"es{dc-1}.{hf}")
            if dc > 1:
                tg.task("pe", p_stat(dc - 2, xh2pan),
                        reads=[f"xh2p{dc-2}", f"sqc{dc % 2}", "onesc"],
                        writes=["b4", "b5", "b6", "b7"], tag=f"ez{dc-2}.{hf}")
            tg.task("sp", ep1_x2w(dc), reads=[f"x2b{dc % 2}"], writes=["x2c"],
                    tag=f"ew{dc}.{hf}")
        tg.task("dve", p_sq(DC - 1, xh2pan), reads=[f"xh2p{DC-1}"],
                writes=[f"sqc{(DC-1) % 2}"], tag=f"es{DC-1}.{hf}")
        tg.task("pe", p_stat(DC - 2, xh2pan),
                reads=[f"xh2p{DC-2}", f"sqc{(DC-2) % 2}", "onesc"],
                writes=["b4", "b5", "b6", "b7"], tag=f"ez{DC-2}.{hf}")
        tg.task("pe", p_stat(DC - 1, xh2pan),
                reads=[f"xh2p{DC-1}", f"sqc{(DC-1) % 2}", "onesc"],
                writes=["b4", "b5", "b6", "b7"], tag=f"ez{DC-1}.{hf}")

        p2_rows(f"l2{hf}")
        p2_bcast(f"l2b{hf}")
        tg.barrier()

        def ep2_xh2(dc):
            def f(e):
                sl = slice(dc * P2H, (dc + 1) * P2H)
                e.tensor_tensor(pls[dc % 2], xh2pan[:, sl], rb_b, ALU.mult)
                return e.tensor_tensor(xh2w[dc % 2], pls[dc % 2], cb_b, ALU.add)
            return f

        def ep2_w(dc):
            return lambda e: e.dma_start(out=xh2c[dc, :, hf * P2H:(hf + 1) * P2H],
                                         in_=xh2w[dc % 2])

        def pass2_step(dc):
            tg.task("dve", ep2_xh2(dc), reads=[f"xh2p{dc}", "rbcb"],
                    writes=[f"xh2w{dc % 2}", "pls", "frAB"], tag=f"eh{dc}.{hf}")
            tg.task("sp", ep2_w(dc), reads=[f"xh2w{dc % 2}"],
                    writes=[f"xh2c{dc}"], tag=f"ev{dc}.{hf}")

        if defer_pass2:
            return pass2_step
        for dc in range(DC):
            pass2_step(dc)

    p2s0 = p2_half(0, defer_pass2=True)
    p2_half(1, eh_hook=p2s0)

    al.reset()
    f1pan = al.take("f1pan", DC * OWN)
    gwb = [al.take(f"gwb{i}", D) for i in range(2)]
    uwb = [al.take(f"uwb{i}", D) for i in range(2)]
    silug = al.take("silug", OWN)
    upv = al.take("upv", OWN)
    hbuf = [al.take(f"hbuf{i}", OWN) for i in range(2)]
    f0b = [al.take(f"f0b{i}", 1376) for i in range(2)]
    assert al.top <= ARENA_N
    f0_jobs = [(fc, q) for fc in range(DC) for q in range(8)]
    f0z = [0]

    def f0_step():
        i = f0z[0]
        if i >= len(f0_jobs):
            return
        f0z[0] += 1
        fc, q = f0_jobs[i]
        tg.task("gp", lambda e: e.dma_start(out=f0b[i % 2],
                                            in_=dw_t[fc, :, q * 1376:(q + 1) * 1376]),
                writes=[f"f0b{i % 2}"], tag=f"f0l{i}")
        tg.task("sp", lambda e: e.dma_start(out=dwc[fc, :, q * 1376:(q + 1) * 1376],
                                            in_=f0b[i % 2]),
                reads=[f"f0b{i % 2}"], writes=["dwc"], tag=f"f0s{i}")

    def f1_wload(which, fc):
        src = gw_t if which == 0 else uw_t
        dst = gwb if which == 0 else uwb
        return lambda e: e.dma_start(out=dst[fc % 2], in_=src[fc, :, :])

    def f1_mm(which, fc):
        wsrc = gwb if which == 0 else uwb
        banks = PBF[0:4] if which == 0 else PBF[4:8]
        def f(t):
            last = None
            for dc in range(DC):
                lh = wsrc[fc % 2][:, dc * 128:(dc + 1) * 128]
                for sub in range(4):
                    last = mm(t, banks[sub], lh,
                              f1pan[:, dc * OWN + sub * 512: dc * OWN + (sub + 1) * 512],
                              dc == 0, dc == DC - 1)
            return last
        return f

    def f1_mm_dc(which, fc, dc):
        wsrc = gwb if which == 0 else uwb
        banks = PBF[0:4] if which == 0 else PBF[4:8]
        def f(t):
            last = None
            lh = wsrc[fc % 2][:, dc * 128:(dc + 1) * 128]
            for sub in range(4):
                last = mm(t, banks[sub], lh,
                          f1pan[:, dc * OWN + sub * 512: dc * OWN + (sub + 1) * 512],
                          dc == 0, dc == DC - 1)
            return last
        return f

    def f1_silu(fc):
        def f(e):
            last = None
            for sub in range(4):
                last = e.activation(silug[:, sub * 512:(sub + 1) * 512], PBF[sub],
                                    AF.Silu, scale=gsc[:, fc:fc + 1])
            return last
        return f

    def f1_up(fc):
        def f(e):
            last = None
            for sub in range(4):
                last = e.activation(upv[:, sub * 512:(sub + 1) * 512], PBF[4 + sub],
                                    AF.Copy, scale=usc[:, fc:fc + 1])
            return last
        return f

    tg.task("gp", f1_wload(1, 0), writes=["uwb0"], tag="f1u0")
    tg.task("gp", f1_wload(1, 1), writes=["uwb1"], tag="f1u1")
    for dc in range(DC):
        if dc < 16:
            war = [f"xpo{2*dc}", f"xpo{2*dc+1}", f"oh{2*dc}", f"oh{2*dc+1}"]
        else:
            war = [f"xh2p{2*(dc-16)}", f"xh2p{2*(dc-16)+1}"]
        tg.task("gp", lambda e, dc=dc: e.dma_start(
            out=f1pan[:, dc * OWN:(dc + 1) * OWN], in_=xh2c[dc, :, :]),
            reads=[f"xh2c{dc}"], writes=["f1pan", f"f1pan{dc}"] + war,
            tag=f"f1ld{dc}")
    if debug:
        tg.task("sp", lambda e: e.dma_start(out=dbg_gw[:, 0:D], in_=gwb[0][:, :]),
                reads=["gwb0"], writes=["dbg"], tag="dgw0")
        tg.task("sp", lambda e: e.dma_start(out=dbg_gw[:, D:2 * D], in_=gwb[1][:, :]),
                reads=["gwb1"], writes=["dbg"], tag="dgw1")
    for fc in range(FFC):
        if 1 < fc + 1 < FFC:
            tg.task("gp", f1_wload(0, fc + 1), writes=[f"gwb{(fc+1) % 2}"],
                    tag=f"f1g{fc+1}")
        if fc + 1 < FFC and fc + 1 > 1:
            tg.task("gp", f1_wload(1, fc + 1), writes=[f"uwb{(fc+1) % 2}"],
                    tag=f"f1u{fc+1}")
        if fc == 0:
            for dc in range(DC):
                tg.task("pe", f1_mm_dc(0, 0, dc), reads=["gwb0", f"f1pan{dc}"],
                        writes=["b0", "b1", "b2", "b3"], tag=f"f1m0.{dc}")
                tg.task("pe", f1_mm_dc(1, 0, dc), reads=["uwb0", f"f1pan{dc}"],
                        writes=["b4", "b5", "b6", "b7"], tag=f"f1n0.{dc}")
            tg.task("act", f1_silu(0), reads=["b0", "b1", "b2", "b3", "gsc"],
                    writes=["silug"], tag="f1s0")
            tg.task("act", f1_up(0), reads=["b4", "b5", "b6", "b7", "usc"],
                    writes=["upv"], tag="f1e0")
        else:
            tg.task("pe", f1_mm(0, fc), reads=[f"gwb{fc % 2}", "f1pan"],
                    writes=["b0", "b1", "b2", "b3"], tag=f"f1m{fc}")
            tg.task("act", f1_silu(fc), reads=["b0", "b1", "b2", "b3", "gsc"],
                    writes=["silug"], tag=f"f1s{fc}")
            tg.task("pe", f1_mm(1, fc), reads=[f"uwb{fc % 2}", "f1pan"],
                    writes=["b4", "b5", "b6", "b7"], tag=f"f1n{fc}")
            tg.task("act", f1_up(fc), reads=["b4", "b5", "b6", "b7", "usc"],
                    writes=["upv"], tag=f"f1e{fc}")
        tg.task("dve", lambda e, fc=fc: e.tensor_tensor(hbuf[fc % 2], silug, upv,
                                                        ALU.mult),
                reads=["silug", "upv"], writes=[f"hbuf{fc % 2}"], tag=f"f1h{fc}")
        tg.task("sp", lambda e, fc=fc: e.dma_start(out=hc[fc, :, :],
                                                   in_=hbuf[fc % 2]),
                reads=[f"hbuf{fc % 2}"], writes=["hc"], tag=f"f1w{fc}")
        f0_step(); f0_step(); f0_step()

    al.reset()
    htile = al.take("htile", FFC * F2T)
    dwb = [al.take(f"dwb{i}", 5504) for i in range(3)]
    dpan = al.take("dpan", DC * F2T)
    x2t = [al.take(f"x2t{i}", 2 * F2T, F32) for i in range(2)]
    outb = [al.take(f"outb{i}", 2 * F2T, F32) for i in range(2)]
    sqf = [al.take(f"sqf{i}", F2T) for i in range(2)]
    tsf = [al.take(f"tsf{i}", 2 * F2T, F32) for i in range(2)]
    brR = al.take("brR", 512, parts=1)
    brC = al.take("brC", 512, parts=1)
    frC = al.take("frC", 1024, F32, parts=1)
    frD = al.take("frD", 1024, F32, parts=1)
    rbS = al.take("rbS", 1024, F32)
    csb = al.take("csb", 1024, F32)
    assert al.top <= ARENA_N

    def f2_stat(t, fc):
        mm(t, PB[2][0:1, 0:512], onesc[:, :], dpan[:, fc * F2T:(fc + 1) * F2T],
           fc == 0, False)
        return mm(t, PB[3][0:1, 0:512], onesc[:, :], sqf[fc % 2],
                  fc == 0, fc == DC - 1)

    def f2_rows(tag):
        def a(e):
            e.activation(frC[0:1, 0:512], PB[2][0:1, 0:512], AF.Copy, scale=1.0 / D)
            return e.activation(frD[0:1, 0:512], PB[3][0:1, 0:512], AF.Copy,
                                scale=1.0 / D)

        def b(e):
            e.tensor_tensor(PB[6][0:1, 0:512], frC[0:1, 0:512], frC[0:1, 0:512],
                            ALU.mult)
            return e.tensor_tensor(PB[7][0:1, 0:512], frD[0:1, 0:512],
                                   PB[6][0:1, 0:512], ALU.subtract)

        def c(e):
            e.activation(PB[5][0:1, 0:512], PB[7][0:1, 0:512], AF.Ln, bias=epsln[:, :])
            e.activation(PB[6][0:1, 0:512], PB[5][0:1, 0:512], AF.Exp, scale=-0.5)
            return e.activation(brR[0:1, 0:512], PB[6][0:1, 0:512], AF.Copy)

        def d(e):
            return e.tensor_tensor(PB[7][0:1, 0:512], frC[0:1, 0:512],
                                   PB[6][0:1, 0:512], ALU.mult)

        def d2(e):
            return e.activation(brC[0:1, 0:512], PB[7][0:1, 0:512], AF.Copy,
                                scale=-1.0)

        def bc(t):
            mm(t, PBF[2], onesr[:, :], brR[0:1, 0:512], True, True)
            return mm(t, PBF[3], onesr[:, :], brC[0:1, 0:512], True, True)

        tg.task("act", a, reads=["b2", "b3"], writes=["frCD"], tag=tag + "a")
        tg.task("dve", b, reads=["frCD"], writes=["b6", "b7"], tag=tag + "b")
        tg.task("act", c, reads=["b7", "epsc"], writes=["b5", "b6", "brR"], tag=tag + "c")
        tg.task("dve", d, reads=["frCD", "b6"], writes=["b7"], tag=tag + "d")
        tg.task("act", d2, reads=["b7"], writes=["brC"], tag=tag + "e")
        tg.task("pe", bc, reads=["brR", "brC", "onesr"], writes=["b2", "b3"],
                tag=tag + "f")

        def bcev(e):
            e.activation(rbS[:, :], PBF[2], AF.Copy)
            return e.activation(csb[:, :], PBF[3], AF.Copy)
        tg.task("act", bcev, reads=["b2", "b3"], writes=["rbcs"], tag=tag + "g")

    def f2_epi_step(tt, dc, q="gp"):
        tg.task(q, lambda e, dc=dc, tt=tt: e.dma_start(
                    out=x2t[dc % 2], in_=x2c[dc, :, tt * F2T:(tt + 1) * F2T]),
                reads=["x2c"], writes=[f"x2t{dc % 2}"], tag=f"f2x{dc}.{tt}")

        def f2_y2(e, dc=dc):
            sl = slice(dc * F2T, (dc + 1) * F2T)
            e.tensor_tensor(tsf[0], dpan[:, sl], rbS[:, :], ALU.mult)
            e.tensor_tensor(tsf[1], tsf[0], csb[:, :], ALU.add)
            return e.tensor_tensor(outb[dc % 2], tsf[1], x2t[dc % 2], ALU.add)
        tg.task("dve", f2_y2, reads=[f"dp{dc}", "rbcs", f"x2t{dc % 2}"],
                writes=[f"outb{dc % 2}", "tsf0", "tsf1"], tag=f"f2y{dc}.{tt}")
        tg.task(q, lambda e, dc=dc, tt=tt: e.dma_start(
                    out=outp[tt, :, dc * F2T:(dc + 1) * F2T], in_=outb[dc % 2]),
                reads=[f"outb{dc % 2}"], writes=["outp"], tag=f"f2o{dc}.{tt}")

    for tt in range(NT2):
        tg.task("sp", lambda e, tt=tt: e.dma_start(
            out=htile[:, :].rearrange("p (c t) -> p c t", c=FFC),
            in_=hc[:, :, tt * F2T:(tt + 1) * F2T].rearrange("c p t -> p c t")),
            reads=["hc"], writes=["htile"], tag=f"f2h{tt}")
        for hh in range(2):
            tg.task("sp", lambda e, hh=hh: e.dma_start(
                        out=dwb[hh], in_=dwc[0, :, hh * 5504:(hh + 1) * 5504]),
                    reads=["dwc"], writes=[f"dwb{hh}"], tag=f"f2w0{hh}.{tt}")
        for fc in range(DC):
            bname = f"b{fc % 2}"
            h0, h1, h2 = (2 * fc) % 3, (2 * fc + 1) % 3, (2 * fc + 2) % 3
            if tt > 0:
                f2_epi_step(tt - 1, fc)
            if fc + 1 < DC:
                tg.task("sp", lambda e, fc=fc, h2=h2: e.dma_start(
                            out=dwb[h2], in_=dwc[fc + 1, :, 0:5504]),
                        reads=["dwc"], writes=[f"dwb{h2}"], tag=f"f2wa{fc+1}.{tt}")

            def f2_mm_a(t, fc=fc, h0=h0):
                last = None
                for c in range(43):
                    last = mm(t, PBF[fc % 2], dwb[h0][:, c * 128:(c + 1) * 128],
                              htile[:, c * F2T:(c + 1) * F2T], c == 0, False)
                return last

            def f2_mm_b(t, fc=fc, h1=h1):
                last = None
                for c in range(43, FFC):
                    last = mm(t, PBF[fc % 2],
                              dwb[h1][:, (c - 43) * 128:(c - 42) * 128],
                              htile[:, c * F2T:(c + 1) * F2T], False, c == FFC - 1)
                return last
            tg.task("pe", f2_mm_a, reads=[f"dwb{h0}", "htile"], writes=[bname],
                    tag=f"f2ma{fc}.{tt}")
            if fc + 1 < DC:
                tg.task("sp", lambda e, fc=fc, h0=h0: e.dma_start(
                            out=dwb[h0], in_=dwc[fc + 1, :, 5504:FF]),
                        reads=["dwc"], writes=[f"dwb{h0}"], tag=f"f2wb{fc+1}.{tt}")
            tg.task("pe", f2_mm_b, reads=[f"dwb{h1}", "htile"], writes=[bname],
                    tag=f"f2mb{fc}.{tt}")
            tg.task("act", lambda e, fc=fc: e.activation(
                        dpan[:, fc * F2T:(fc + 1) * F2T], PBF[fc % 2], AF.Copy,
                        scale=dsc[:, fc:fc + 1]),
                    reads=[bname, "dsc"], writes=[f"dp{fc}"], tag=f"f2d{fc}.{tt}")
            tg.task("dve", lambda e, fc=fc: e.tensor_tensor(
                        sqf[fc % 2], dpan[:, fc * F2T:(fc + 1) * F2T],
                        dpan[:, fc * F2T:(fc + 1) * F2T], ALU.mult),
                    reads=[f"dp{fc}"], writes=[f"sqf{fc % 2}"], tag=f"f2s{fc}.{tt}")
            if fc > 0:
                tg.task("pe", lambda t, fc=fc: f2_stat(t, fc - 1),
                        reads=[f"dp{fc-1}", f"sqf{(fc-1) % 2}", "onesc"],
                        writes=["b2", "b3"], tag=f"f2t{fc-1}.{tt}")
        tg.task("pe", lambda t: f2_stat(t, DC - 1),
                reads=[f"dp{DC-1}", f"sqf{(DC-1) % 2}", "onesc"],
                writes=["b2", "b3"], tag=f"f2t{DC-1}.{tt}")
        f2_rows(f"f2r{tt}")
    for dc in range(DC):
        f2_epi_step(NT2 - 1, dc, q="gp" if dc % 2 == 0 else "sp")

    tg.emit_into(blk)
    st.close()
    return nc


_NC_CACHE = {}


def kernel(_debug=False, **inputs):
    key = "ncd" if _debug else "nc"
    if key not in _NC_CACHE:
        _NC_CACHE[key] = build_program(debug=_debug)
    nc = _NC_CACHE[key]

    x = np.ascontiguousarray(np.asarray(inputs["L0_attn_hidden_states"], np.float32))
    shared = dict(
        wq_t=prep_w_fc(np.asarray(inputs["L0_attn_Wq"], np.float32)),
        wo_t=prep_w_fc(np.asarray(inputs["L0_attn_Wo"], np.float32)),
        wk_t=prep_wkv(np.asarray(inputs["L0_attn_Wk"], np.float32)),
        wv_t=prep_wkv(np.asarray(inputs["L0_attn_Wv"], np.float32)),
        gw_t=prep_w_fc(np.asarray(inputs["L0_ffn_gate_w"], np.int32)),
        uw_t=prep_w_fc(np.asarray(inputs["L0_ffn_up_w"], np.int32)),
        dw_t=prep_wd(np.asarray(inputs["L0_ffn_down_w"], np.int32)),
        bq_c=prep_col(np.asarray(inputs["L0_attn_bq"], np.float32)),
        bo_c=prep_col(np.asarray(inputs["L0_attn_bo"], np.float32)),
        bk_r=np.ascontiguousarray(
            np.asarray(inputs["L0_attn_bk"], np.float32).reshape(1, FK)),
        bv_r=np.ascontiguousarray(
            np.asarray(inputs["L0_attn_bv"], np.float32).reshape(1, FK)),
        gs_c=prep_col(np.asarray(inputs["L0_ffn_gate_s"], np.float32).reshape(-1)),
        us_c=prep_col(np.asarray(inputs["L0_ffn_up_s"], np.float32).reshape(-1)),
        ds_c=prep_col(np.asarray(inputs["L0_ffn_down_s"], np.float32).reshape(-1)),
    )
    in_maps = []
    for core in range(NCORES):
        b, half = core // 2, core % 2
        m = dict(shared)
        m["xs1"] = prep_x_s1(x[b], half)
        m["xp2"] = prep_x_p2(x[b], half)
        in_maps.append(m)

    res = run_bass_kernel_spmd(nc, in_maps, core_ids=list(range(NCORES)))
    if _debug:
        return res
    return gather_out([r["outp"] for r in res.results])
